# version 01
# baseline (speedup 1.0000x reference)
import numpy as np
import ml_dtypes
from contextlib import ExitStack
import concourse.bass as bass
import concourse.mybir as mybir
from concourse.bass_utils import run_bass_kernel_spmd

F32 = mybir.dt.float32
BF16 = mybir.dt.bfloat16
OP = mybir.AluOpType
AF = mybir.ActivationFunctionType
AX = mybir.AxisListType
bf16 = ml_dtypes.bfloat16

S = 2048
D = 1024
H = 16
HD = 64
NCK = 8
NTT = 16
NE = 8
EPS_GATE = 1e-4
W_THRES = 0.1
LN_EPS = 1e-5
NEG = -1e9
BIG = 1e9

ENGS = ("tensor", "vector", "scalar", "sync", "gpsimd")


class _Sem:
    def __init__(self, name):
        self.name = name
        self.n = 0
        self.h = None


def _ap_iv(x):
    if not (hasattr(x, "tensor") and hasattr(x, "offset") and hasattr(x, "ap")):
        return None
    try:
        esz = mybir.dt.size(x.dtype)
        dims = list(x.ap)
        ext = 1
        for step, count in dims[1:]:
            ext += abs(int(step)) * (int(count) - 1)
        lo = int(x.offset) * esz
        return (x.tensor.name, lo, lo + ext * esz)
    except Exception:
        return None


_HAZ_ENGS = ("vector", "scalar", "gpsimd")
_NO_TRACK = {"dma_start", "nop", "drain", "memzero"}
_WINDOW = 12


class Builder:
    def __init__(self, nc, lenient=False):
        self.nc = nc
        self.lenient = lenient
        self.code = {e: [] for e in ENGS}
        self.events = {}
        self.sems = {}
        self.bank_free = {}
        self.recent = {e: [] for e in _HAZ_ENGS}
        self.n_drains = 0

    def sem(self, name):
        s = _Sem(name)
        self.sems[name] = s
        return s

    def op(self, eng, meth, *a, inc=None, **kw):
        if eng in _HAZ_ENGS and meth not in _NO_TRACK:
            writes, reads = [], []
            for i, x in enumerate(a):
                iv = _ap_iv(x)
                if iv is None:
                    continue
                (writes if i == 0 else reads).append(iv)
            for k2, x in kw.items():
                iv = _ap_iv(x)
                if iv is None:
                    continue
                (writes if k2 in ("out", "accum_out") else reads).append(iv)
            if meth == "memset":
                pass
            hit = False
            for nm, lo, hi in writes + reads:
                for onm, olo, ohi in self.recent[eng]:
                    if nm == onm and lo < ohi and olo < hi:
                        hit = True
                        break
                if hit:
                    break
            if hit:
                self.code[eng].append(("op", "drain", (), {}, None))
                self.recent[eng] = []
                self.n_drains += 1
            self.recent[eng].extend(writes)
            self.recent[eng] = self.recent[eng][-_WINDOW:]
        if inc is not None:
            sem, k = inc
            sem.n += k
        self.code[eng].append(("op", meth, a, kw, inc))

    def wait(self, eng, ev):
        assert isinstance(ev, str)
        self.code[eng].append(("wait", ev))

    def wait_val(self, eng, sem, val):
        self.code[eng].append(("waitv", sem, val))

    def mark(self, ev, sem):
        assert ev not in self.events, ev
        self.events[ev] = (sem, sem.n)

    def bank_acquire(self, eng, idx):
        ev = self.bank_free.get(idx)
        if ev is not None:
            self.wait(eng, ev)

    def bank_release(self, idx, ev):
        self.bank_free[idx] = ev

    def playback(self, eng_name, eng):
        for item in self.code[eng_name]:
            if item[0] == "op":
                _, meth, a, kw, inc = item
                inst = getattr(eng, meth)(*a, **kw)
                if inc is not None:
                    sem, k = inc
                    inst.then_inc(sem.h, k)
            elif item[0] == "wait":
                if item[1] not in self.events:
                    assert self.lenient, item[1]
                    continue
                sem, val = self.events[item[1]]
                if val > 0:
                    eng.wait_ge(sem.h, val)
            else:
                _, sem, val = item
                if val > 0:
                    eng.wait_ge(sem.h, val)


def ceil_div(a, b):
    return (a + b - 1) // b


def build_program(upto=None, dumps=(), dbg_h1=False, dbg_nores=False):
    nc = bass.Bass("TRN2", target_bir_lowering=False)
    B = Builder(nc, lenient=upto is not None)

    dx = nc.dram_tensor("x", [S, D], F32, kind="ExternalInput").ap()
    dwq = nc.dram_tensor("wq", [D, D], F32, kind="ExternalInput").ap()
    dwk = nc.dram_tensor("wk", [D, D], F32, kind="ExternalInput").ap()
    dwv = nc.dram_tensor("wv", [D, D], F32, kind="ExternalInput").ap()
    dwo = nc.dram_tensor("wo", [D, D], F32, kind="ExternalInput").ap()
    doutb = nc.dram_tensor("outb", [1, D], F32, kind="ExternalInput").ap()
    dgw = nc.dram_tensor("gw", [D, NE], F32, kind="ExternalInput").ap()
    dgb = nc.dram_tensor("gb", [1, NE], F32, kind="ExternalInput").ap()
    dew = nc.dram_tensor("ew", [NE, D, D], F32, kind="ExternalInput").ap()
    deb = nc.dram_tensor("eb", [NE, D], F32, kind="ExternalInput").ap()
    didf = nc.dram_tensor("identf", [128, 128], F32, kind="ExternalInput").ap()
    didb = nc.dram_tensor("identb", [128, 128], BF16, kind="ExternalInput").ap()
    dmask = nc.dram_tensor("dmask", [128, 128], BF16, kind="ExternalInput").ap()
    dsel = nc.dram_tensor("sel", [128, 128], BF16, kind="ExternalInput").ap()
    didx = nc.dram_tensor("idx8", [128, NE], F32, kind="ExternalInput").ap()
    dy = nc.dram_tensor("y", [S, D], F32, kind="ExternalOutput").ap()

    ctx = ExitStack()
    SB = lambda name, shape, dt: ctx.enter_context(nc.sbuf_tensor(name, shape, dt))

    A_w = SB("A_w", [128, 16384], BF16)
    A_qk = SB("A_qk", [128, 32768], BF16)
    A_v = SB("A_v", [128, 16384], BF16)
    A_xa = SB("A_xa", [128, 32768], BF16)

    def carve(arena, byte_off, nparts, dt, nfree):
        if dt == BF16:
            assert byte_off % 2 == 0
            return arena[0:nparts, byte_off // 2: byte_off // 2 + nfree]
        assert byte_off % 4 == 0
        return arena[:, :].bitcast(F32)[0:nparts, byte_off // 4: byte_off // 4 + nfree]

    wslot = [carve(A_w, i * 16384, 128, BF16, 8192).rearrange(
        "p (c o) -> p c o", c=NCK) for i in range(2)]
    Qfm = [carve(A_qk, c * 4096, 128, BF16, 2048) for c in range(NCK)]
    Kfm = [carve(A_qk, 32768 + c * 4096, 128, BF16, 2048) for c in range(NCK)]
    x1_tm = [carve(A_qk, tt * 4096, 128, F32, 1024) for tt in range(NTT)]
    Vtm = [carve(A_v, tt * 2048, 128, BF16, 1024) for tt in range(NTT)]
    xbstage = [carve(A_v, i * 2048, 128, BF16, 1024) for i in range(2)]
    x1fm = [carve(A_v, c * 4096, 128, BF16, 2048) for c in range(NCK)]
    xfm = [carve(A_xa, c * 4096, 128, BF16, 2048) for c in range(NCK)]
    afm = xfm
    WK = 32768
    Ps = (carve(A_xa, WK + 0, 128, BF16, 2048),
          carve(A_xa, WK + 4096, 128, BF16, 2048))
    PTs = (carve(A_xa, WK + 8192, 128, BF16, 2048),
           carve(A_xa, WK + 12288, 128, BF16, 2048))
    qsq = carve(A_xa, WK + 16384, 128, BF16, 2048)
    qkn2 = carve(A_xa, WK + 20480, 16, F32, 2048)
    xstage = [carve(A_xa, WK + 16384 + i * 4096, 128, F32, 1024)
              for i in range(2)]
    x1bfst = [carve(A_xa, WK + 24576 + i * 2048, 128, BF16, 1024)
              for i in range(2)]
    rcpT = [carve(A_xa, WK + 28672 + i * 2048, 1, F32, 512) for i in range(2)]
    acc = [carve(A_xa, tt * 4096, 128, F32, 1024) for tt in range(NTT)]

    identf = SB("identf_sb", [128, 128], F32)
    identb = SB("identb_sb", [128, 128], BF16)
    dmaskb = SB("dmask_sb", [128, 128], BF16)
    selb = SB("sel_sb", [128, 128], BF16)
    idx8 = SB("idx8_sb", [128, NE], F32)
    ones_f = SB("ones_f", [1, 128], F32)
    ones_b = SB("ones_b", [1, 128], BF16)
    outb_row = SB("outb_row", [1, D], BF16)
    gw_sb = SB("gw_sb", [128, NCK * NE], BF16)
    gb_row = SB("gb_row", [1, NE], BF16)
    eb_sb = SB("eb_sb", [NE, D], BF16)
    bias_all = SB("bias_all", [128, 256], F32)
    qn_tm = SB("qn_tm", [128, 256], F32)
    maxk_row = SB("maxk_row", [1, 16], F32)
    maxk_bc = SB("maxk_bc", [128, 16], F32)
    rsums = SB("rsums", [128, 32], F32)
    rparts = SB("rparts", [128, 8], F32)
    combT = SB("combT", [NE, S], BF16)
    scores_g = SB("scores_g", [128, NTT * NE], F32)
    p_g = SB("p_g", [128, NE], F32)
    comb_g = SB("comb_g", [128, NTT * NE], F32)
    gsc = SB("gsc", [128, 16], F32)
    g8a = SB("g8a", [128, NE], F32)
    g8b = SB("g8b", [128, NE], F32)
    lnst = SB("lnst", [128, 8], F32)
    combbf = SB("combbf", [128, NE], BF16)
    dbgst = SB("dbgst", [128, 128], F32)

    bank = [ctx.enter_context(nc.psum_tensor(f"bank{i}", [128, 512], F32))
            for i in range(8)]

    def bb(i):
        return bank[i][:, :].bitcast(BF16)

    sDg = B.sem("sDg")
    sDs = B.sem("sDs")
    sPE = B.sem("sPE")
    sV = B.sem("sV")
    sA = B.sem("sA")

    def g_dma(out, in_, ev=None):
        B.op("gpsimd", "dma_start", out=out, in_=in_, inc=(sDg, 16))
        if ev:
            B.mark(ev, sDg)

    def s_dma(out, in_, ev=None):
        B.op("sync", "dma_start", out=out, in_=in_, inc=(sDs, 16))
        if ev:
            B.mark(ev, sDs)


    def _finish_debug():
        B.wait_val("sync", sPE, sPE.n)
        B.wait_val("sync", sV, sV.n)
        B.wait_val("sync", sA, sA.n)
        B.wait_val("sync", sDg, sDg.n)
        dump_views = {
            "xfm": xfm, "Qfm": Qfm, "Kfm": Kfm, "Vtm": Vtm, "afm": afm,
            "x1_tm": x1_tm, "x1fm": x1fm, "acc": acc,
            "bias_all": [bias_all[:, :]], "qn_tm": [qn_tm[:, :]],
            "maxk_bc": [maxk_bc[:, :]], "rsums": [rsums[:, :]],
            "comb_g": [comb_g[:, :]], "combT": [combT[:, :]],
            "scores_g": [scores_g[:, :]], "dbgst": [dbgst[:, :]],
            "P0": [Ps[0][:, :]], "P1": [Ps[1][:, :]],
            "PT0": [PTs[0][:, :]], "PT1": [PTs[1][:, :]],
        }
        for dn in dumps:
            views = dump_views[dn]
            for vi, ap in enumerate(views):
                shp = list(ap.shape)
                dt = ap.dtype
                dout = nc.dram_tensor(f"dump_{dn}_{vi}", shp, dt,
                                      kind="ExternalOutput").ap()
                s_dma(dout, ap)
        B.wait_val("sync", sDs, sDs.n)

    def _playback_and_close():
        for s in B.sems.values():
            s.h = ctx.enter_context(nc.semaphore(s.name))
        with nc.Block() as block:
            @block.tensor
            def _(t):
                B.playback("tensor", t)

            @block.vector
            def _(v):
                B.playback("vector", v)

            @block.scalar
            def _(s):
                B.playback("scalar", s)

            @block.sync
            def _(s):
                B.playback("sync", s)

            @block.gpsimd
            def _(g):
                B.playback("gpsimd", g)
        ctx.close()
        return nc

    s_dma(identf[:, :], didf)
    s_dma(identb[:, :], didb)
    s_dma(dmaskb[:, :], dmask)
    s_dma(selb[:, :], dsel)
    s_dma(idx8[:, :], didx, ev="consts")
    B.op("gpsimd", "memset", ones_f[:, :], 1.0)
    B.op("gpsimd", "memset", ones_b[:, :], 1.0, inc=(sDg, 1))
    B.mark("ones", sDg)

    for tt in range(NTT):
        if tt >= 2:
            B.wait("gpsimd", f"xtr{tt - 2}")
        g_dma(xbstage[tt % 2][:, :], dx[tt * 128:(tt + 1) * 128, :],
              ev=f"xld{tt}")
    g_dma(wslot[0][:, :, :], dwq.rearrange("(c p) o -> p c o", p=128), ev="wq")
    g_dma(wslot[1][:, :, :], dwk.rearrange("(c p) o -> p c o", p=128), ev="wk")
    g_dma(gw_sb[:, :].rearrange("p (c e) -> p c e", c=NCK),
          dgw.rearrange("(c p) e -> p c e", p=128))
    g_dma(gb_row[:, :], dgb)
    g_dma(outb_row[:, :], doutb)
    g_dma(eb_sb[:, :], deb, ev="smallw")
    B.wait("gpsimd", "qmm_done")
    g_dma(wslot[0][:, :, :], dwv.rearrange("(c p) o -> p c o", p=128), ev="wv")
    B.wait("gpsimd", "kmm_done")
    g_dma(wslot[1][:, :, :], dwo.rearrange("(c p) o -> p c o", p=128), ev="wo")
    for e in range(NE):
        if e == 0:
            B.wait("gpsimd", "vmm_done")
        elif e == 1:
            B.wait("gpsimd", "omm_done")
        else:
            B.wait("gpsimd", f"expmm{e - 2}")
        g_dma(wslot[e % 2][:, :, :],
              dew[e].rearrange("(c p) o -> p c o", p=128), ev=f"we{e}")

    B.wait("tensor", "ones")
    B.wait("tensor", "consts")
    for tt in range(NTT):
        B.wait("tensor", f"xld{tt}")
        for c in range(NCK):
            i = 6 + c // 4
            if c % 4 == 0:
                B.bank_acquire("tensor", i)
            B.op("tensor", "transpose", bb(i)[:, (c % 4) * 128:(c % 4) * 128 + 128],
                 xbstage[tt % 2][:, c * 128:(c + 1) * 128], identb[:, :],
                 inc=(sPE, 1) if c == NCK - 1 else None)
        B.mark(f"xtr{tt}", sPE)
        B.wait("vector", f"xtr{tt}")
        for half in range(2):
            i = 6 + half
            B.op("vector", "tensor_copy",
                 A_xa[:, :].rearrange("p (c t) -> p c t", c=16)
                 [:, half * 4:half * 4 + 4, tt * 128:(tt + 1) * 128],
                 bb(i)[:, 0:512].rearrange("p (c t) -> p c t", c=4),
                 inc=(sV, 1))
            B.mark(f"xev{tt}_{half}", sV)
            B.bank_release(i, f"xev{tt}_{half}")

    if upto == "xfm":
        _finish_debug()
        return _playback_and_close()
    def proj_fm(name, slot_i, dst, w_ev):
        B.wait("tensor", w_ev)
        bi = 0
        for oc in range(NCK):
            for tg in range(4):
                bk = bank[bi % 4]
                B.bank_acquire("tensor", bi % 4)
                for dc in range(NCK):
                    B.op("tensor", "matmul", bk[:, :],
                         wslot[slot_i][:, dc, oc * 128:(oc + 1) * 128],
                         xfm[dc][:, tg * 512:(tg + 1) * 512],
                         start=(dc == 0), stop=(dc == NCK - 1),
                         inc=(sPE, 1) if dc == NCK - 1 else None)
                B.mark(f"{name}mm{bi}", sPE)
                B.wait("vector", f"{name}mm{bi}")
                B.op("vector", "tensor_copy",
                     dst[oc][:, tg * 512:(tg + 1) * 512], bk[:, :], inc=(sV, 1))
                B.mark(f"{name}ev{bi}", sV)
                B.bank_release(bi % 4, f"{name}ev{bi}")
                bi += 1
        B.mark(f"{name}mm_done", sPE)
        B.mark(f"{name}ev_done", sV)

    B.wait("tensor", f"xev{NTT - 1}_1")
    proj_fm("q", 0, Qfm, "wq")
    proj_fm("k", 1, Kfm, "wk")

    B.wait("tensor", "wv")
    bi = 0
    for tt in range(NTT):
        for dog in range(2):
            bk = bank[bi % 4]
            B.bank_acquire("tensor", bi % 4)
            for dc in range(NCK):
                B.op("tensor", "matmul", bk[:, :],
                     xfm[dc][:, tt * 128:(tt + 1) * 128],
                     wslot[0][:, dc, dog * 512:(dog + 1) * 512],
                     start=(dc == 0), stop=(dc == NCK - 1),
                     inc=(sPE, 1) if dc == NCK - 1 else None)
            B.mark(f"vmm{bi}", sPE)
            B.wait("vector", f"vmm{bi}")
            B.op("vector", "tensor_copy",
                 Vtm[tt][:, dog * 512:(dog + 1) * 512], bk[:, :], inc=(sV, 1))
            B.mark(f"vev{bi}", sV)
            B.bank_release(bi % 4, f"vev{bi}")
            bi += 1
    B.mark("vmm_done", sPE)

    if upto == "qkv":
        _finish_debug()
        return _playback_and_close()
    def side_norms(name, src, src_ev_v, prev_pe_ev):
        B.wait("scalar", src_ev_v)
        if prev_pe_ev is not None:
            B.wait("scalar", prev_pe_ev)
        for c in range(NCK):
            B.op("scalar", "activation", qsq[:, :], src[c][:, :], AF.Square,
                 inc=(sA, 1))
            B.mark(f"{name}sq{c}", sA)
            if c + 1 < NCK:
                B.wait("scalar", f"{name}n2c{c}")
            B.wait("tensor", f"{name}sq{c}")
            for tg in range(4):
                if c == 0:
                    B.bank_acquire("tensor", 4 + tg)
                B.op("tensor", "matmul", bank[4 + tg][0:16, :],
                     selb[:, c * 16:(c + 1) * 16],
                     qsq[:, tg * 512:(tg + 1) * 512],
                     start=(c == 0), stop=(c == NCK - 1),
                     inc=(sPE, 1) if tg == 3 else None)
            B.mark(f"{name}n2c{c}", sPE)
        B.wait("vector", f"{name}n2c{NCK - 1}")
        for tg in range(4):
            B.op("vector", "tensor_copy", qkn2[:, tg * 512:(tg + 1) * 512],
                 bank[4 + tg][0:16, :], inc=(sV, 1))
            B.mark(f"{name}n2e{tg}", sV)
            B.bank_release(4 + tg, f"{name}n2e{tg}")
        B.wait("scalar", f"{name}n2e3")
        B.op("scalar", "activation", qkn2[:, :], qkn2[:, :], AF.Sqrt,
             inc=(sA, 1))
        B.mark(f"{name}n", sA)

    side_norms("k", Kfm, "kev_done", None)
    B.wait("vector", "kn")
    B.op("vector", "reduce_max", gsc[0:16, 0:1], qkn2[:, :], axis=AX.X,
         inc=(sV, 1))
    B.mark("maxk", sV)
    B.wait("tensor", "maxk")
    B.bank_acquire("tensor", 4)
    B.op("tensor", "transpose", bank[4][0:1, 0:16], gsc[0:16, 0:1],
         identf[0:16, 0:16], inc=(sPE, 1))
    B.mark("maxkT", sPE)
    B.wait("vector", "maxkT")
    B.op("vector", "tensor_copy", maxk_row[:, :], bank[4][0:1, 0:16],
         inc=(sV, 1))
    B.mark("maxkrow", sV)
    B.bank_release(4, "maxkrow")
    B.wait("tensor", "maxkrow")
    B.bank_acquire("tensor", 5)
    B.op("tensor", "matmul", bank[5][:, 0:16], ones_f[:, :], maxk_row[:, :],
         start=True, stop=True, inc=(sPE, 1))
    B.mark("maxkbcmm", sPE)
    B.wait("vector", "maxkbcmm")
    B.op("vector", "tensor_copy", maxk_bc[:, :], bank[5][:, 0:16], inc=(sV, 1))
    B.mark("maxkbc", sV)
    B.bank_release(5, "maxkbc")

    side_norms("q", Qfm, "qev_done", "kn2c7")
    B.wait("tensor", "qn")
    B.bank_acquire("tensor", 6)
    for qt in range(NTT):
        B.op("tensor", "transpose", bank[6][:, qt * 16:(qt + 1) * 16],
             qkn2[:, qt * 128:(qt + 1) * 128], identf[0:16, 0:16],
             inc=(sPE, 1) if qt == NTT - 1 else None)
    B.mark("qnT", sPE)
    B.wait("vector", "qnT")
    B.wait("vector", "maxkbc")
    B.op("vector", "tensor_copy", qn_tm[:, :], bank[6][:, 0:256], inc=(sV, 1))
    B.mark("qn_tm_ev", sV)
    B.bank_release(6, "qn_tm_ev")
    B.op("vector", "scalar_tensor_tensor",
         bias_all[:, :].rearrange("p (a b) -> p a b", a=16),
         qn_tm[:, :].rearrange("p (a b) -> p a b", a=16),
         -0.125,
         maxk_bc[:, :].unsqueeze(1).broadcast_to([128, 16, 16]),
         op0=OP.mult, op1=OP.mult, inc=(sV, 1))
    B.mark("bias_all", sV)

    if upto == "norms":
        _finish_debug()
        return _playback_and_close()
    B.wait("scalar", "bias_all")
    for hp in range(NCK):
        h0 = 2 * hp
        for qt in range(NTT):
            nk = qt + 1
            ng = ceil_div(nk, 4)
            for g in range(ng):
                w = 512 if g < ng - 1 else (qt % 4 + 1) * 128
                for hh in range(2):
                    i = hh * 2 + (g % 2)
                    bk = bank[i]
                    diag = (g == ng - 1)
                    B.bank_acquire("tensor", i)
                    B.op("tensor", "matmul", bk[:, 0:w],
                         Qfm[hp][hh * 64:(hh + 1) * 64,
                                 qt * 128:(qt + 1) * 128],
                         Kfm[hp][hh * 64:(hh + 1) * 64, g * 512:g * 512 + w],
                         start=True, stop=not diag,
                         tile_position=(hh * 64, 0),
                         inc=None if diag else (sPE, 1))
                    if diag:
                        B.op("tensor", "matmul", bk[:, w - 128:w],
                             identb[:, :], dmaskb[:, :],
                             start=False, stop=True, inc=(sPE, 1))
                    B.mark(f"sc_{hp}_{qt}_{g}_{hh}", sPE)
                    B.wait("scalar", f"sc_{hp}_{qt}_{g}_{hh}")
                    if qt > 0 and g == 0 and hh == 0:
                        B.wait("scalar", f"rsum_{hp}_{qt - 1}")
                    elif hp > 0 and qt == 0 and g == 0 and hh == 0:
                        B.wait("scalar", f"rsum_{hp - 1}_{NTT - 1}")
                    B.op("scalar", "activation",
                         Ps[hh][:, g * 512:g * 512 + w], bk[:, 0:w], AF.Exp,
                         bias=bias_all[:, qt * 16 + h0 + hh:
                                       qt * 16 + h0 + hh + 1],
                         scale=0.125,
                         accum_out=rparts[:, hh * 4 + g:hh * 4 + g + 1],
                         inc=(sA, 1))
                    B.mark(f"exp_{hp}_{qt}_{g}_{hh}", sA)
                    B.bank_release(i, f"exp_{hp}_{qt}_{g}_{hh}")
            B.wait("vector", f"exp_{hp}_{qt}_{ng - 1}_0")
            B.wait("vector", f"exp_{hp}_{qt}_{ng - 1}_1")
            for hh in range(2):
                B.op("vector", "tensor_reduce",
                     rsums[:, hh * 16 + qt:hh * 16 + qt + 1],
                     rparts[:, hh * 4:hh * 4 + ng], axis=AX.X, op=OP.add,
                     inc=(sV, 1) if hh == 1 else None)
            B.mark(f"rsum_{hp}_{qt}", sV)
            for hh in range(2):
                for kc in range(nk):
                    gi = kc // 4
                    i = 4 + (gi % 2)
                    if kc % 4 == 0:
                        B.bank_acquire("tensor", i)
                        B.wait("tensor", f"exp_{hp}_{qt}_{gi}_{hh}")
                    B.op("tensor", "transpose",
                         bb(i)[:, (kc % 4) * 128:(kc % 4) * 128 + 128],
                         Ps[hh][:, kc * 128:(kc + 1) * 128], identb[:, :],
                         inc=(sPE, 1) if (kc % 4 == 3 or kc == nk - 1) else None)
                    if kc % 4 == 3 or kc == nk - 1:
                        B.mark(f"pt_{hp}_{qt}_{hh}_{gi}", sPE)
                        B.wait("vector", f"pt_{hp}_{qt}_{hh}_{gi}")
                        wv_ = (kc % 4 + 1) * 128
                        B.op("vector", "tensor_copy",
                             PTs[hh][:, gi * 512:gi * 512 + wv_],
                             bb(i)[:, 0:wv_], inc=(sV, 1))
                        B.mark(f"ptev_{hp}_{qt}_{hh}_{gi}", sV)
                        B.bank_release(i, f"ptev_{hp}_{qt}_{hh}_{gi}")
            i = 6 + (qt % 2)
            pvb = bank[i]
            B.bank_acquire("tensor", i)
            B.wait("tensor", f"ptev_{hp}_{qt}_1_{(nk - 1) // 4}")
            for hh in range(2):
                for kc in range(nk):
                    B.op("tensor", "matmul", pvb[hh * 64:(hh + 1) * 64, 0:128],
                         Vtm[kc][:, (h0 + hh) * 64:(h0 + hh + 1) * 64],
                         PTs[hh][:, kc * 128:(kc + 1) * 128],
                         start=(kc == 0), stop=(kc == nk - 1),
                         tile_position=(0, hh * 64),
                         inc=(sPE, 1) if (kc == nk - 1 and hh == 1) else None)
            B.mark(f"pvmm_{hp}_{qt}", sPE)
            B.wait("vector", f"pvmm_{hp}_{qt}")
            B.op("vector", "tensor_copy",
                 afm[hp][:, qt * 128:(qt + 1) * 128], pvb[:, 0:128],
                 inc=(sV, 1))
            B.mark(f"pvev_{hp}_{qt}", sV)
            B.bank_release(i, f"pvev_{hp}_{qt}")
        B.op("vector", "reciprocal", rsums[:, :], rsums[:, :], inc=(sV, 1))
        B.mark(f"rcp_{hp}", sV)
        B.wait("tensor", f"rcp_{hp}")
        rcp_last_user = {}
        for hh in range(2):
            for g4 in range(4):
                par = (hh * 4 + g4) % 2
                i = 4 + par
                B.bank_acquire("tensor", i)
                for j in range(4):
                    qt = g4 * 4 + j
                    B.op("tensor", "transpose",
                         bank[i][0:1, j * 128:(j + 1) * 128],
                         rsums[:, hh * 16 + qt:hh * 16 + qt + 1],
                         identf[:, :], inc=(sPE, 1) if j == 3 else None)
                B.mark(f"rcT_{hp}_{hh}_{g4}", sPE)
                B.wait("vector", f"rcT_{hp}_{hh}_{g4}")
                if par in rcp_last_user:
                    B.wait("vector", rcp_last_user[par])
                B.op("vector", "tensor_copy", rcpT[par][0:1, :],
                     bank[i][0:1, 0:512], inc=(sV, 1))
                B.mark(f"rcE_{hp}_{hh}_{g4}", sV)
                B.bank_release(i, f"rcE_{hp}_{hh}_{g4}")
                B.wait("tensor", f"rcE_{hp}_{hh}_{g4}")
                ib = par
                B.bank_acquire("tensor", ib)
                B.op("tensor", "matmul", bank[ib][:, :], ones_f[:, :],
                     rcpT[par][0:1, :], start=True, stop=True, inc=(sPE, 1))
                B.mark(f"rcB_{hp}_{hh}_{g4}", sPE)
                rcp_last_user[par] = f"rcB_{hp}_{hh}_{g4}"
                B.wait("vector", f"rcB_{hp}_{hh}_{g4}")
                B.op("vector", "tensor_tensor",
                     afm[hp][hh * 64:(hh + 1) * 64, g4 * 512:(g4 + 1) * 512],
                     afm[hp][hh * 64:(hh + 1) * 64, g4 * 512:(g4 + 1) * 512],
                     bank[ib][hh * 64:(hh + 1) * 64, :], op=OP.mult,
                     inc=(sV, 1))
                B.mark(f"nrm_{hp}_{hh}_{g4}", sV)
                B.bank_release(ib, f"nrm_{hp}_{hh}_{g4}")
        B.mark(f"attn_{hp}", sV)

    if upto == "attn":
        _finish_debug()
        return _playback_and_close()
    B.wait("tensor", f"attn_{NCK - 1}")
    B.wait("tensor", "wo")
    B.wait("sync", "qn")
    for tt in range(NTT):
        if tt >= 2:
            B.wait("sync", f"h1_{tt - 2}")
        s_dma(xstage[tt % 2][:, :], dx[tt * 128:(tt + 1) * 128, :],
              ev=f"xf32_{tt}")
    bi = 0
    for tt in range(NTT):
        for dog in range(2):
            i = bi % 4
            bk = bank[i]
            B.bank_acquire("tensor", i)
            for dc in range(NCK):
                B.op("tensor", "matmul", bk[:, :],
                     afm[dc][:, tt * 128:(tt + 1) * 128],
                     wslot[1][:, dc, dog * 512:(dog + 1) * 512],
                     start=(dc == 0), stop=False)
            B.op("tensor", "matmul", bk[:, :], ones_b[:, :],
                 outb_row[0:1, dog * 512:(dog + 1) * 512],
                 start=False, stop=True, inc=(sPE, 1))
            B.mark(f"omm{bi}", sPE)
            B.wait("vector", f"omm{bi}")
            if dog == 0:
                B.wait("vector", f"xf32_{tt}")
            if dbg_nores:
                B.op("vector", "tensor_scalar",
                     x1_tm[tt][:, dog * 512:(dog + 1) * 512], bk[:, :],
                     1.0, 0.0, op0=OP.mult, op1=OP.add,
                     accum_out=lnst[:, dog:dog + 1], inc=(sV, 1))
            else:
                B.op("vector", "scalar_tensor_tensor",
                     x1_tm[tt][:, dog * 512:(dog + 1) * 512], bk[:, :], 1.0,
                     xstage[tt % 2][:, dog * 512:(dog + 1) * 512],
                     op0=OP.mult, op1=OP.add,
                     accum_out=lnst[:, dog:dog + 1],
                     inc=(sV, 1))
            B.mark(f"h1ev{bi}", sV)
            B.bank_release(i, f"h1ev{bi}")
            bi += 1
        B.mark(f"h1_{tt}", sV)
        if dbg_h1 and tt < 2:
            dh = nc.dram_tensor(f"dbg_h1_{tt}", [128, D], F32,
                                kind="ExternalOutput").ap()
            dxs = nc.dram_tensor(f"dbg_xs_{tt}", [128, D], F32,
                                 kind="ExternalOutput").ap()
            B.wait("scalar", f"h1_{tt}")
            B.op("scalar", "dma_start", out=dh, in_=x1_tm[tt][:, :],
                 inc=(sDs, 16))
            B.op("scalar", "dma_start", out=dxs, in_=xstage[tt % 2][:, :],
                 inc=(sDs, 16))
            B.wait_val("vector", sDs, sDs.n)
        B.wait("scalar", f"h1_{tt}")
        B.op("scalar", "activation", x1bfst[tt % 2][:, :], x1_tm[tt][:, :],
             AF.Square, accum_out=lnst[:, 2:3], inc=(sA, 1))
        B.mark(f"sq1_{tt}", sA)
        B.wait("vector", f"sq1_{tt}")
        B.op("vector", "tensor_reduce", lnst[:, 3:4], lnst[:, 0:2], axis=AX.X,
             op=OP.add)
        B.op("vector", "tensor_scalar", lnst[:, 3:4], lnst[:, 3:4], 1.0 / D,
             None, op0=OP.mult)
        B.op("vector", "tensor_scalar", lnst[:, 4:5], lnst[:, 3:4],
             lnst[:, 3:4], None, op0=OP.mult)
        B.op("vector", "scalar_tensor_tensor", lnst[:, 5:6], lnst[:, 2:3],
             1.0 / D, lnst[:, 4:5], op0=OP.mult, op1=OP.subtract)
        B.op("vector", "tensor_scalar", lnst[:, 5:6], lnst[:, 5:6], LN_EPS,
             None, op0=OP.add)
        B.op("vector", "reciprocal", lnst[:, 6:7], lnst[:, 5:6], inc=(sV, 1))
        B.mark(f"st1a_{tt}", sV)
        B.wait("scalar", f"st1a_{tt}")
        B.op("scalar", "activation", lnst[:, 7:8], lnst[:, 6:7], AF.Sqrt,
             inc=(sA, 1))
        B.mark(f"st1b_{tt}", sA)
        B.wait("vector", f"st1b_{tt}")
        if not dbg_nores:
            B.op("vector", "tensor_scalar", x1_tm[tt][:, :], x1_tm[tt][:, :],
                 lnst[:, 3:4], lnst[:, 7:8], op0=OP.subtract, op1=OP.mult)
        B.op("vector", "tensor_copy", x1bfst[tt % 2][:, :], x1_tm[tt][:, :],
             inc=(sV, 1))
        B.mark(f"x1b_{tt}", sV)
        B.wait("tensor", f"x1b_{tt}")
        for c in range(NCK):
            i = 4 + c // 4
            if c % 4 == 0:
                B.bank_acquire("tensor", i)
            B.op("tensor", "transpose",
                 bb(i)[:, (c % 4) * 128:(c % 4) * 128 + 128],
                 x1bfst[tt % 2][:, c * 128:(c + 1) * 128], identb[:, :],
                 inc=(sPE, 1) if c == NCK - 1 else None)
        B.mark(f"x1T_{tt}", sPE)
        B.wait("vector", f"x1T_{tt}")
        for half in range(2):
            i = 4 + half
            B.op("vector", "tensor_copy",
                 A_v[:, :].rearrange("p (c t) -> p c t", c=8)
                 [:, half * 4:half * 4 + 4, tt * 128:(tt + 1) * 128],
                 bb(i)[:, 0:512].rearrange("p (c t) -> p c t", c=4),
                 inc=(sV, 1))
            B.mark(f"x1fev_{tt}_{half}", sV)
            B.bank_release(i, f"x1fev_{tt}_{half}")
    B.mark("omm_done", sPE)

    if upto == "ln1":
        _finish_debug()
        return _playback_and_close()
    B.wait("tensor", "smallw")
    for tt in range(NTT):
        B.wait("tensor", f"x1fev_{tt}_1")
        i = 6
        B.bank_acquire("tensor", i)
        for dc in range(NCK):
            B.op("tensor", "matmul", bank[i][:, 0:NE],
                 x1fm[dc][:, tt * 128:(tt + 1) * 128],
                 gw_sb[:, dc * NE:(dc + 1) * NE],
                 start=(dc == 0), stop=False)
        B.op("tensor", "matmul", bank[i][:, 0:NE], ones_b[:, :],
             gb_row[0:1, :], start=False, stop=True, inc=(sPE, 1))
        B.mark(f"gmm_{tt}", sPE)
        B.wait("vector", f"gmm_{tt}")
        sc = scores_g[:, tt * NE:(tt + 1) * NE]
        B.op("vector", "tensor_scalar", sc, bank[i][:, 0:NE], EPS_GATE, None,
             op0=OP.max, inc=(sV, 1))
        B.mark(f"gev_{tt}", sV)
        B.bank_release(i, f"gev_{tt}")
        B.op("vector", "reduce_max", gsc[:, 0:1], sc, axis=AX.X, negate=True,
             inc=(sV, 1))
        B.mark(f"gmx_{tt}", sV)
        B.wait("scalar", f"gmx_{tt}")
        B.op("scalar", "activation", p_g[:, :], sc, AF.Exp,
             bias=gsc[:, 0:1], scale=1.0, accum_out=gsc[:, 1:2], inc=(sA, 1))
        B.mark(f"gexp_{tt}", sA)
        B.wait("vector", f"gexp_{tt}")
        B.op("vector", "reciprocal", gsc[:, 2:3], gsc[:, 1:2])
        B.op("vector", "tensor_scalar", p_g[:, :], p_g[:, :], gsc[:, 2:3],
             None, op0=OP.mult)
        B.op("vector", "reduce_max", gsc[:, 3:4], p_g[:, :], axis=AX.X,
             negate=True)
        B.op("vector", "tensor_scalar", g8a[:, :], p_g[:, :], gsc[:, 3:4],
             None, op0=OP.add)
        B.op("vector", "tensor_single_scalar", g8a[:, :], g8a[:, :], 0.0,
             OP.is_ge)
        B.op("vector", "tensor_scalar", g8a[:, :], g8a[:, :], -BIG, BIG,
             op0=OP.mult, op1=OP.add)
        B.op("vector", "tensor_tensor", g8a[:, :], g8a[:, :], idx8[:, :],
             op=OP.add)
        B.op("vector", "tensor_reduce", gsc[:, 4:5], g8a[:, :], axis=AX.X,
             op=OP.min, negate=True)
        B.op("vector", "tensor_scalar", g8a[:, :], idx8[:, :], gsc[:, 4:5],
             None, op0=OP.add)
        B.op("vector", "tensor_single_scalar", g8a[:, :], g8a[:, :], 0.0,
             OP.is_equal)
        B.op("vector", "scalar_tensor_tensor", g8b[:, :], g8a[:, :], -BIG,
             p_g[:, :], op0=OP.mult, op1=OP.add)
        B.op("vector", "reduce_max", gsc[:, 5:6], g8b[:, :], axis=AX.X,
             negate=True)
        B.op("vector", "tensor_scalar", g8b[:, :], g8b[:, :], gsc[:, 5:6],
             None, op0=OP.add)
        B.op("vector", "tensor_single_scalar", g8b[:, :], g8b[:, :], 0.0,
             OP.is_ge)
        B.op("vector", "tensor_scalar", g8b[:, :], g8b[:, :], -BIG, BIG,
             op0=OP.mult, op1=OP.add)
        B.op("vector", "tensor_tensor", g8b[:, :], g8b[:, :], idx8[:, :],
             op=OP.add)
        B.op("vector", "tensor_reduce", gsc[:, 6:7], g8b[:, :], axis=AX.X,
             op=OP.min, negate=True)
        B.op("vector", "tensor_scalar", g8b[:, :], idx8[:, :], gsc[:, 6:7],
             None, op0=OP.add)
        B.op("vector", "tensor_single_scalar", g8b[:, :], g8b[:, :], 0.0,
             OP.is_equal)
        B.op("vector", "tensor_tensor", g8a[:, :], g8a[:, :], g8b[:, :],
             op=OP.add)
        cb = comb_g[:, tt * NE:(tt + 1) * NE]
        B.op("vector", "tensor_scalar", cb, p_g[:, :], W_THRES, None,
             op0=OP.max)
        B.op("vector", "tensor_tensor", cb, cb, g8a[:, :], op=OP.mult)
        B.op("vector", "tensor_reduce", gsc[:, 7:8], cb, axis=AX.X, op=OP.add)
        B.op("vector", "reciprocal", gsc[:, 8:9], gsc[:, 7:8])
        B.op("vector", "tensor_scalar", cb, cb, gsc[:, 8:9], 0.5,
             op0=OP.mult, op1=OP.mult)
        B.op("vector", "tensor_copy", combbf[:, :], cb, inc=(sV, 1))
        B.mark(f"comb_{tt}", sV)
        B.wait("tensor", f"comb_{tt}")
        i = 6 + tt % 2
        B.bank_acquire("tensor", i)
        B.op("tensor", "transpose", bb(i)[0:NE, 0:128], combbf[:, :],
             identb[:, :], inc=(sPE, 1))
        B.mark(f"combT_{tt}", sPE)
        B.wait("vector", f"combT_{tt}")
        B.op("vector", "tensor_copy", combT[:, tt * 128:(tt + 1) * 128],
             bb(i)[0:NE, 0:128], inc=(sV, 1))
        B.mark(f"combTev_{tt}", sV)
        B.bank_release(i, f"combTev_{tt}")

    if upto == "gate":
        _finish_debug()
        return _playback_and_close()
    B.wait("tensor", "smallw")
    bi = 0
    for tt in range(NTT):
        B.wait("tensor", f"combTev_{tt}")
        if tt == 14:
            B.wait("vector", f"sq1_{NTT - 1}")
        for dog in range(2):
            i = bi % 4
            B.bank_acquire("tensor", i)
            B.op("tensor", "matmul", bank[i][:, :],
                 combT[:, tt * 128:(tt + 1) * 128],
                 eb_sb[:, dog * 512:(dog + 1) * 512],
                 start=True, stop=True, inc=(sPE, 1))
            B.mark(f"bmm{bi}", sPE)
            B.wait("vector", f"bmm{bi}")
            B.op("vector", "tensor_copy",
                 acc[tt][:, dog * 512:(dog + 1) * 512], bank[i][:, :],
                 inc=(sV, 1))
            B.mark(f"bev{bi}", sV)
            B.bank_release(i, f"bev{bi}")
            bi += 1

    bi = 0
    for e in range(NE):
        B.wait("tensor", f"we{e}")
        ws = wslot[e % 2]
        for tt in range(NTT):
            for dog in range(2):
                i = bi % 6
                B.bank_acquire("tensor", i)
                for dc in range(NCK):
                    B.op("tensor", "matmul", bank[i][:, :],
                         x1fm[dc][:, tt * 128:(tt + 1) * 128],
                         ws[:, dc, dog * 512:(dog + 1) * 512],
                         start=(dc == 0), stop=(dc == NCK - 1),
                         inc=(sPE, 1) if dc == NCK - 1 else None)
                B.mark(f"emm{bi}", sPE)
                B.wait("vector", f"emm{bi}")
                B.op("vector", "scalar_tensor_tensor",
                     acc[tt][:, dog * 512:(dog + 1) * 512], bank[i][:, :],
                     comb_g[:, tt * NE + e:tt * NE + e + 1],
                     acc[tt][:, dog * 512:(dog + 1) * 512],
                     op0=OP.mult, op1=OP.add, inc=(sV, 1))
                B.mark(f"eev{bi}", sV)
                B.bank_release(i, f"eev{bi}")
                bi += 1
        B.mark(f"expmm{e}", sPE)

    if upto == "ffn":
        _finish_debug()
        return _playback_and_close()
    for tt in range(NTT):
        B.op("vector", "scalar_tensor_tensor", acc[tt][:, 0:512],
             acc[tt][:, 0:512], 1.0, x1_tm[tt][:, 0:512],
             op0=OP.mult, op1=OP.add, accum_out=lnst[:, 0:1])
        B.op("vector", "scalar_tensor_tensor", acc[tt][:, 512:1024],
             acc[tt][:, 512:1024], 1.0, x1_tm[tt][:, 512:1024],
             op0=OP.mult, op1=OP.add, accum_out=lnst[:, 1:2], inc=(sV, 1))
        B.mark(f"h2_{tt}", sV)
        B.wait("scalar", f"h2_{tt}")
        if tt == 0:
            B.bank_acquire("scalar", 6)
            B.bank_acquire("scalar", 7)
        B.op("scalar", "activation", bank[6][:, :], acc[tt][:, 0:512],
             AF.Square, accum_out=lnst[:, 2:3])
        B.op("scalar", "activation", bank[7][:, :], acc[tt][:, 512:1024],
             AF.Square, accum_out=lnst[:, 4:5], inc=(sA, 1))
        B.mark(f"sq2_{tt}", sA)
        B.wait("vector", f"sq2_{tt}")
        B.op("vector", "tensor_reduce", lnst[:, 3:4], lnst[:, 0:2], axis=AX.X,
             op=OP.add)
        B.op("vector", "tensor_scalar", lnst[:, 3:4], lnst[:, 3:4], 1.0 / D,
             None, op0=OP.mult)
        B.op("vector", "tensor_tensor", lnst[:, 2:3], lnst[:, 2:3],
             lnst[:, 4:5], op=OP.add)
        B.op("vector", "tensor_scalar", lnst[:, 4:5], lnst[:, 3:4],
             lnst[:, 3:4], None, op0=OP.mult)
        B.op("vector", "scalar_tensor_tensor", lnst[:, 5:6], lnst[:, 2:3],
             1.0 / D, lnst[:, 4:5], op0=OP.mult, op1=OP.subtract)
        B.op("vector", "tensor_scalar", lnst[:, 5:6], lnst[:, 5:6], LN_EPS,
             None, op0=OP.add)
        B.op("vector", "reciprocal", lnst[:, 6:7], lnst[:, 5:6], inc=(sV, 1))
        B.mark(f"st2a_{tt}", sV)
        B.wait("scalar", f"st2a_{tt}")
        B.op("scalar", "activation", lnst[:, 7:8], lnst[:, 6:7], AF.Sqrt,
             inc=(sA, 1))
        B.mark(f"st2b_{tt}", sA)
        B.wait("vector", f"st2b_{tt}")
        B.op("vector", "tensor_scalar", acc[tt][:, :], acc[tt][:, :],
             lnst[:, 3:4], lnst[:, 7:8], op0=OP.subtract, op1=OP.mult,
             inc=(sV, 1))
        B.mark(f"out_{tt}", sV)
        B.wait("sync", f"out_{tt}")
        s_dma(dy[tt * 128:(tt + 1) * 128, :], acc[tt][:, :], ev=f"st_{tt}")
    B.wait_val("sync", sDs, sDs.n)

    return _playback_and_close()


_NC_CACHE = {}


def _get_nc():
    if "nc" not in _NC_CACHE:
        _NC_CACHE["nc"] = build_program()
    return _NC_CACHE["nc"]


def _consts():
    identf = np.eye(128, dtype=np.float32)
    identb = identf.astype(bf16)
    q = np.arange(128)[:, None]
    k = np.arange(128)[None, :]
    dmask = np.where(k > q, NEG, 0.0).astype(np.float32).astype(bf16)
    sel = np.zeros((128, 128), np.float32)
    for c in range(8):
        sel[0:64, c * 16 + 2 * c] = 1.0
        sel[64:128, c * 16 + 2 * c + 1] = 1.0
    selb = sel.astype(bf16)
    idx8 = np.broadcast_to(np.arange(8, dtype=np.float32), (128, 8)).copy()
    return identf, identb, dmask, selb, idx8


def make_in_maps(x, wq, wk, wv, wo, outb, gw, gb, ew, eb):
    identf, identb, dmask, selb, idx8 = _consts()
    in_maps = []
    for i in range(8):
        in_maps.append({
            "x": np.ascontiguousarray(x[i]),
            "wq": wq, "wk": wk, "wv": wv, "wo": wo, "outb": outb,
            "gw": gw, "gb": gb, "ew": ew, "eb": eb,
            "identf": identf, "identb": identb, "dmask": dmask,
            "sel": selb, "idx8": idx8,
        })
    return in_maps


def kernel(L0_attn_batch, L0_attn_w_query, L0_attn_w_key, L0_attn_w_value,
           L0_attn_out_w, L0_attn_out_b, L0_ffn_gate_W, L0_ffn_gate_b,
           L0_ffn_expert_W, L0_ffn_expert_b):
    x = np.ascontiguousarray(np.asarray(L0_attn_batch, dtype=np.float32))
    wq = np.ascontiguousarray(np.asarray(L0_attn_w_query, dtype=np.float32))
    wk = np.ascontiguousarray(np.asarray(L0_attn_w_key, dtype=np.float32))
    wv = np.ascontiguousarray(np.asarray(L0_attn_w_value, dtype=np.float32))
    wo = np.ascontiguousarray(np.asarray(L0_attn_out_w, dtype=np.float32))
    outb = np.ascontiguousarray(
        np.asarray(L0_attn_out_b, dtype=np.float32)).reshape(1, D)
    gw = np.ascontiguousarray(np.asarray(L0_ffn_gate_W, dtype=np.float32))
    gb = np.ascontiguousarray(
        np.asarray(L0_ffn_gate_b, dtype=np.float32)).reshape(1, NE)
    ew = np.ascontiguousarray(np.asarray(L0_ffn_expert_W, dtype=np.float32))
    eb = np.ascontiguousarray(np.asarray(L0_ffn_expert_b, dtype=np.float32))
    nc = _get_nc()
    in_maps = make_in_maps(x, wq, wk, wv, wo, outb, gw, gb, ew, eb)
    res = run_bass_kernel_spmd(nc, in_maps, core_ids=list(range(8)))
    out = np.stack([np.asarray(r["y"], dtype=np.float32)
                    for r in res.results], axis=0)
    return out


# version 02
# speedup vs baseline: 1.0895x; 1.0895x over previous
import numpy as np
import ml_dtypes
from contextlib import ExitStack
import concourse.bass as bass
import concourse.mybir as mybir
from concourse.bass_utils import run_bass_kernel_spmd

F32 = mybir.dt.float32
BF16 = mybir.dt.bfloat16
OP = mybir.AluOpType
AF = mybir.ActivationFunctionType
AX = mybir.AxisListType
bf16 = ml_dtypes.bfloat16

S = 2048
D = 1024
H = 16
HD = 64
NCK = 8
NTT = 16
NE = 8
EPS_GATE = 1e-4
W_THRES = 0.1
LN_EPS = 1e-5
NEG = -1e9
BIG = 1e9

ENGS = ("tensor", "vector", "scalar", "sync", "gpsimd")


class _Sem:
    def __init__(self, name):
        self.name = name
        self.n = 0
        self.h = None


def _ap_iv(x):
    if not (hasattr(x, "tensor") and hasattr(x, "offset") and hasattr(x, "ap")):
        return None
    try:
        esz = mybir.dt.size(x.dtype)
        dims = list(x.ap)
        ext = 1
        for step, count in dims[1:]:
            ext += abs(int(step)) * (int(count) - 1)
        lo = int(x.offset) * esz
        return (x.tensor.name, lo, lo + ext * esz)
    except Exception:
        return None


_HAZ_ENGS = ("vector", "scalar", "gpsimd")
_NO_TRACK = {"dma_start", "nop", "drain", "memzero"}
_WINDOW = 12


class Builder:
    def __init__(self, nc, lenient=False):
        self.nc = nc
        self.lenient = lenient
        self.code = {e: [] for e in ENGS}
        self.events = {}
        self.sems = {}
        self.bank_free = {}
        self.recent = {e: [] for e in _HAZ_ENGS}
        self.n_drains = 0

    def sem(self, name):
        s = _Sem(name)
        self.sems[name] = s
        return s

    def op(self, eng, meth, *a, inc=None, **kw):
        if eng in _HAZ_ENGS and meth not in _NO_TRACK:
            writes, reads = [], []
            for i, x in enumerate(a):
                iv = _ap_iv(x)
                if iv is None:
                    continue
                (writes if i == 0 else reads).append(iv)
            for k2, x in kw.items():
                iv = _ap_iv(x)
                if iv is None:
                    continue
                (writes if k2 in ("out", "accum_out") else reads).append(iv)
            if meth == "memset":
                pass
            hit = False
            for wr, nm, lo, hi in ([(1, *w) for w in writes] +
                                   [(0, *r) for r in reads]):
                for owr, onm, olo, ohi in self.recent[eng]:
                    if (wr or owr) and nm == onm and lo < ohi and olo < hi:
                        hit = True
                        break
                if hit:
                    break
            if hit:
                self.code[eng].append(("op", "drain", (), {}, None))
                self.recent[eng] = []
                self.n_drains += 1
            self.recent[eng].extend([(1, *w) for w in writes] +
                                    [(0, *r) for r in reads])
            self.recent[eng] = self.recent[eng][-_WINDOW:]
        if inc is not None:
            sem, k = inc
            sem.n += k
        self.code[eng].append(("op", meth, a, kw, inc))

    def wait(self, eng, ev):
        assert isinstance(ev, str)
        self.code[eng].append(("wait", ev))

    def wait_val(self, eng, sem, val):
        self.code[eng].append(("waitv", sem, val))

    def mark(self, ev, sem):
        assert ev not in self.events, ev
        self.events[ev] = (sem, sem.n)

    def bank_acquire(self, eng, idx):
        ev = self.bank_free.get(idx)
        if ev is not None:
            self.wait(eng, ev)

    def bank_release(self, idx, ev):
        self.bank_free[idx] = ev

    def playback(self, eng_name, eng):
        for item in self.code[eng_name]:
            if item[0] == "op":
                _, meth, a, kw, inc = item
                inst = getattr(eng, meth)(*a, **kw)
                if inc is not None:
                    sem, k = inc
                    inst.then_inc(sem.h, k)
            elif item[0] == "wait":
                if item[1] not in self.events:
                    assert self.lenient, item[1]
                    continue
                sem, val = self.events[item[1]]
                if val > 0:
                    eng.wait_ge(sem.h, val)
            else:
                _, sem, val = item
                if val > 0:
                    eng.wait_ge(sem.h, val)


def ceil_div(a, b):
    return (a + b - 1) // b


def build_program(upto=None, dumps=(), dbg_h1=False, dbg_nores=False):
    nc = bass.Bass("TRN2", target_bir_lowering=False)
    B = Builder(nc, lenient=upto is not None)

    dx = nc.dram_tensor("x", [S, D], F32, kind="ExternalInput").ap()
    dwq = nc.dram_tensor("wq", [D, D], F32, kind="ExternalInput").ap()
    dwk = nc.dram_tensor("wk", [D, D], F32, kind="ExternalInput").ap()
    dwv = nc.dram_tensor("wv", [D, D], F32, kind="ExternalInput").ap()
    dwo = nc.dram_tensor("wo", [D, D], F32, kind="ExternalInput").ap()
    doutb = nc.dram_tensor("outb", [1, D], F32, kind="ExternalInput").ap()
    dgw = nc.dram_tensor("gw", [D, NE], F32, kind="ExternalInput").ap()
    dgb = nc.dram_tensor("gb", [1, NE], F32, kind="ExternalInput").ap()
    dew = nc.dram_tensor("ew", [NE, D, D], F32, kind="ExternalInput").ap()
    deb = nc.dram_tensor("eb", [NE, D], F32, kind="ExternalInput").ap()
    didf = nc.dram_tensor("identf", [128, 128], F32, kind="ExternalInput").ap()
    didb = nc.dram_tensor("identb", [128, 128], BF16, kind="ExternalInput").ap()
    dmask = nc.dram_tensor("dmask", [128, 128], BF16, kind="ExternalInput").ap()
    dsel = nc.dram_tensor("sel", [128, 128], BF16, kind="ExternalInput").ap()
    didx = nc.dram_tensor("idx8", [128, NE], F32, kind="ExternalInput").ap()
    dy = nc.dram_tensor("y", [S, D], F32, kind="ExternalOutput").ap()

    ctx = ExitStack()
    SB = lambda name, shape, dt: ctx.enter_context(nc.sbuf_tensor(name, shape, dt))

    A_w = SB("A_w", [128, 16384], BF16)
    A_qk = SB("A_qk", [128, 32768], BF16)
    A_v = SB("A_v", [128, 16640], BF16)
    A_xa = SB("A_xa", [128, 32768], BF16)

    def carve(arena, byte_off, nparts, dt, nfree):
        if dt == BF16:
            assert byte_off % 2 == 0
            return arena[0:nparts, byte_off // 2: byte_off // 2 + nfree]
        assert byte_off % 4 == 0
        return arena[:, :].bitcast(F32)[0:nparts, byte_off // 4: byte_off // 4 + nfree]

    wslot = [carve(A_w, i * 16384, 128, BF16, 8192).rearrange(
        "p (c o) -> p c o", c=NCK) for i in range(2)]
    Qfm = [carve(A_qk, c * 4096, 128, BF16, 2048) for c in range(NCK)]
    Kfm = [carve(A_qk, 32768 + c * 4096, 128, BF16, 2048) for c in range(NCK)]
    x1_tm = [carve(A_qk, tt * 4096, 128, F32, 1024) for tt in range(NTT)]
    Vtm = [carve(A_v, tt * 2080, 128, BF16, 1040) for tt in range(NTT)]
    xbstage = [carve(A_v, i * 2048, 128, BF16, 1024) for i in range(2)]
    x1fm = [carve(A_v, c * 4096, 128, BF16, 2048) for c in range(NCK)]
    xfm = [carve(A_xa, c * 4096, 128, BF16, 2048) for c in range(NCK)]
    afm = xfm
    WK = 32768
    piece = [[[carve(A_xa, WK + (((ps * 2 + hh) * 2 + pr) * 1024), 128, BF16,
                     512) for pr in range(2)] for hh in range(2)]
             for ps in range(2)]
    attn_tm = [carve(A_xa, WK + 8192 + ps * 4096, 128, BF16, 2048)
               for ps in range(2)]
    qsq = carve(A_xa, WK + 16384, 128, BF16, 2048)
    qkn2 = carve(A_xa, WK + 20480, 16, F32, 2048)
    xstage = [carve(A_xa, WK + 16384 + i * 4096, 128, F32, 1024)
              for i in range(2)]
    x1bfst = [carve(A_xa, WK + 24576 + i * 2048, 128, BF16, 1024)
              for i in range(2)]
    acc = [carve(A_xa, tt * 4096, 128, F32, 1024) for tt in range(NTT)]

    identf = SB("identf_sb", [128, 128], F32)
    identb = SB("identb_sb", [128, 128], BF16)
    dmaskb = SB("dmask_sb", [128, 128], BF16)
    selb = SB("sel_sb", [128, 128], BF16)
    idx8 = SB("idx8_sb", [128, NE], F32)
    ones_f = SB("ones_f", [1, 128], F32)
    ones_b = SB("ones_b", [1, 128], BF16)
    outb_row = SB("outb_row", [1, D], BF16)
    gw_sb = SB("gw_sb", [128, NCK * NE], BF16)
    gb_row = SB("gb_row", [1, NE], BF16)
    eb_sb = SB("eb_sb", [NE, D], BF16)
    bias_hd = SB("bias_hd", [128, 16], F32)
    maxk_row = SB("maxk_row", [1, 16], F32)
    zerosb = SB("zerosb", [128, 128], BF16)
    rcsc = SB("rcsc", [128, 4], F32)
    combT = SB("combT", [NE, S], BF16)
    scores_g = SB("scores_g", [128, NTT * NE], F32)
    p_g = SB("p_g", [128, NE], F32)
    comb_g = SB("comb_g", [128, NTT * NE], F32)
    gsc = SB("gsc", [128, 16], F32)
    g8a = SB("g8a", [128, NE], F32)
    g8b = SB("g8b", [128, NE], F32)
    lnst = SB("lnst", [128, 8], F32)
    combbf = SB("combbf", [128, NE], BF16)
    dbgst = SB("dbgst", [128, 128], F32)

    bank = [ctx.enter_context(nc.psum_tensor(f"bank{i}", [128, 512], F32))
            for i in range(8)]

    def bb(i):
        return bank[i][:, :].bitcast(BF16)

    sDg = B.sem("sDg")
    sDs = B.sem("sDs")
    sPE = B.sem("sPE")
    sV = B.sem("sV")
    sA = B.sem("sA")

    def g_dma(out, in_, ev=None):
        B.op("gpsimd", "dma_start", out=out, in_=in_, inc=(sDg, 16))
        if ev:
            B.mark(ev, sDg)

    def s_dma(out, in_, ev=None):
        B.op("sync", "dma_start", out=out, in_=in_, inc=(sDs, 16))
        if ev:
            B.mark(ev, sDs)


    def _finish_debug():
        B.wait_val("sync", sPE, sPE.n)
        B.wait_val("sync", sV, sV.n)
        B.wait_val("sync", sA, sA.n)
        B.wait_val("sync", sDg, sDg.n)
        dump_views = {
            "xfm": xfm, "Qfm": Qfm, "Kfm": Kfm, "Vtm": Vtm, "afm": afm,
            "x1_tm": x1_tm, "x1fm": x1fm, "acc": acc,
            "bias_all": [bias_hd[:, :]],
            "comb_g": [comb_g[:, :]], "combT": [combT[:, :]],
            "scores_g": [scores_g[:, :]], "dbgst": [dbgst[:, :]],
        }
        for dn in dumps:
            views = dump_views[dn]
            for vi, ap in enumerate(views):
                shp = list(ap.shape)
                dt = ap.dtype
                dout = nc.dram_tensor(f"dump_{dn}_{vi}", shp, dt,
                                      kind="ExternalOutput").ap()
                s_dma(dout, ap)
        B.wait_val("sync", sDs, sDs.n)

    def _playback_and_close():
        for s in B.sems.values():
            s.h = ctx.enter_context(nc.semaphore(s.name))
        with nc.Block() as block:
            @block.tensor
            def _(t):
                B.playback("tensor", t)

            @block.vector
            def _(v):
                B.playback("vector", v)

            @block.scalar
            def _(s):
                B.playback("scalar", s)

            @block.sync
            def _(s):
                B.playback("sync", s)

            @block.gpsimd
            def _(g):
                B.playback("gpsimd", g)
        ctx.close()
        return nc

    s_dma(identf[:, :], didf)
    s_dma(identb[:, :], didb)
    s_dma(dmaskb[:, :], dmask)
    s_dma(selb[:, :], dsel)
    s_dma(idx8[:, :], didx, ev="consts")
    B.op("gpsimd", "memset", ones_f[:, :], 1.0)
    B.op("gpsimd", "memset", zerosb[:, :], 0.0)
    B.op("gpsimd", "memset", ones_b[:, :], 1.0, inc=(sDg, 1))
    B.mark("ones", sDg)

    for tt in range(NTT):
        if tt >= 2:
            B.wait("gpsimd", f"xtr{tt - 2}")
        g_dma(xbstage[tt % 2][:, :], dx[tt * 128:(tt + 1) * 128, :],
              ev=f"xld{tt}")
    g_dma(wslot[0][:, :, :], dwq.rearrange("(c p) o -> p c o", p=128), ev="wq")
    g_dma(wslot[1][:, :, :], dwk.rearrange("(c p) o -> p c o", p=128), ev="wk")
    g_dma(gw_sb[:, :].rearrange("p (c e) -> p c e", c=NCK),
          dgw.rearrange("(c p) e -> p c e", p=128))
    g_dma(gb_row[:, :], dgb)
    g_dma(outb_row[:, :], doutb)
    g_dma(eb_sb[:, :], deb, ev="smallw")
    B.wait("gpsimd", "qmm_done")
    g_dma(wslot[0][:, :, :], dwv.rearrange("(c p) o -> p c o", p=128), ev="wv")
    B.wait("gpsimd", "kmm_done")
    g_dma(wslot[1][:, :, :], dwo.rearrange("(c p) o -> p c o", p=128), ev="wo")
    B.wait("gpsimd", f"xtr{NTT - 1}")
    B.op("gpsimd", "memset",
         A_v[:, :].rearrange("p (t x) -> p t x", t=NTT)[:, :, 64:1040:65],
         1.0, inc=(sDg, 1))
    B.mark("vones", sDg)
    for e in range(NE):
        if e == 0:
            B.wait("gpsimd", "vmm_done")
        elif e == 1:
            B.wait("gpsimd", "omm_done")
        else:
            B.wait("gpsimd", f"expmm{e - 2}")
        g_dma(wslot[e % 2][:, :, :],
              dew[e].rearrange("(c p) o -> p c o", p=128), ev=f"we{e}")

    B.wait("tensor", "ones")
    B.wait("tensor", "consts")
    for tt in range(NTT):
        B.wait("tensor", f"xld{tt}")
        for c in range(NCK):
            i = 6 + c // 4
            if c % 4 == 0:
                B.bank_acquire("tensor", i)
            B.op("tensor", "transpose", bb(i)[:, (c % 4) * 128:(c % 4) * 128 + 128],
                 xbstage[tt % 2][:, c * 128:(c + 1) * 128], identb[:, :],
                 inc=(sPE, 1) if c == NCK - 1 else None)
        B.mark(f"xtr{tt}", sPE)
        B.wait("vector", f"xtr{tt}")
        for half in range(2):
            i = 6 + half
            B.op("vector", "tensor_copy",
                 A_xa[:, :].rearrange("p (c t) -> p c t", c=16)
                 [:, half * 4:half * 4 + 4, tt * 128:(tt + 1) * 128],
                 bb(i)[:, 0:512].rearrange("p (c t) -> p c t", c=4),
                 inc=(sV, 1))
            B.mark(f"xev{tt}_{half}", sV)
            B.bank_release(i, f"xev{tt}_{half}")

    if upto == "xfm":
        _finish_debug()
        return _playback_and_close()
    def proj_fm(name, slot_i, dst, w_ev):
        B.wait("tensor", w_ev)
        bi = 0
        for oc in range(NCK):
            for tg in range(4):
                bk = bank[bi % 4]
                B.bank_acquire("tensor", bi % 4)
                for dc in range(NCK):
                    B.op("tensor", "matmul", bk[:, :],
                         wslot[slot_i][:, dc, oc * 128:(oc + 1) * 128],
                         xfm[dc][:, tg * 512:(tg + 1) * 512],
                         start=(dc == 0), stop=(dc == NCK - 1),
                         inc=(sPE, 1) if dc == NCK - 1 else None)
                B.mark(f"{name}mm{bi}", sPE)
                B.wait("vector", f"{name}mm{bi}")
                B.op("vector", "tensor_copy",
                     dst[oc][:, tg * 512:(tg + 1) * 512], bk[:, :], inc=(sV, 1))
                B.mark(f"{name}ev{bi}", sV)
                B.bank_release(bi % 4, f"{name}ev{bi}")
                bi += 1
        B.mark(f"{name}mm_done", sPE)
        B.mark(f"{name}ev_done", sV)

    B.wait("tensor", f"xev{NTT - 1}_1")
    proj_fm("q", 0, Qfm, "wq")
    proj_fm("k", 1, Kfm, "wk")

    B.wait("tensor", "wv")
    bi = 0
    for tt in range(NTT):
        for dog in range(2):
            bk = bank[bi % 4]
            B.bank_acquire("tensor", bi % 4)
            for dc in range(NCK):
                B.op("tensor", "matmul", bk[:, :],
                     xfm[dc][:, tt * 128:(tt + 1) * 128],
                     wslot[0][:, dc, dog * 512:(dog + 1) * 512],
                     start=(dc == 0), stop=(dc == NCK - 1),
                     inc=(sPE, 1) if dc == NCK - 1 else None)
            B.mark(f"vmm{bi}", sPE)
            B.wait("vector", f"vmm{bi}")
            B.op("vector", "tensor_copy",
                 Vtm[tt][:, dog * 8 * 65:dog * 8 * 65 + 520]
                 .rearrange("p (h d) -> p h d", h=8)[:, :, 0:64],
                 bk[:, :].rearrange("p (h d) -> p h d", h=8), inc=(sV, 1))
            B.mark(f"vev{bi}", sV)
            B.bank_release(bi % 4, f"vev{bi}")
            bi += 1
    B.mark("vmm_done", sPE)

    if upto == "qkv":
        _finish_debug()
        return _playback_and_close()
    def side_norms(name, src, src_ev_v, prev_pe_ev):
        B.wait("scalar", src_ev_v)
        if prev_pe_ev is not None:
            B.wait("scalar", prev_pe_ev)
        for c in range(NCK):
            B.op("scalar", "activation", qsq[:, :], src[c][:, :], AF.Square,
                 inc=(sA, 1))
            B.mark(f"{name}sq{c}", sA)
            if c + 1 < NCK:
                B.wait("scalar", f"{name}n2c{c}")
            B.wait("tensor", f"{name}sq{c}")
            for tg in range(4):
                if c == 0:
                    B.bank_acquire("tensor", 4 + tg)
                B.op("tensor", "matmul", bank[4 + tg][0:16, :],
                     selb[:, c * 16:(c + 1) * 16],
                     qsq[:, tg * 512:(tg + 1) * 512],
                     start=(c == 0), stop=(c == NCK - 1),
                     inc=(sPE, 1) if tg == 3 else None)
            B.mark(f"{name}n2c{c}", sPE)
        B.wait("vector", f"{name}n2c{NCK - 1}")
        for tg in range(4):
            B.op("vector", "tensor_copy", qkn2[:, tg * 512:(tg + 1) * 512],
                 bank[4 + tg][0:16, :], inc=(sV, 1))
            B.mark(f"{name}n2e{tg}", sV)
            B.bank_release(4 + tg, f"{name}n2e{tg}")
        B.wait("scalar", f"{name}n2e3")
        B.op("scalar", "activation", qkn2[:, :], qkn2[:, :], AF.Sqrt,
             inc=(sA, 1))
        B.mark(f"{name}n", sA)

    side_norms("q", Qfm, "qev_done", None)
    B.wait("vector", "qn")
    B.op("vector", "reduce_max", gsc[0:16, 0:1], qkn2[:, :], axis=AX.X,
         inc=(sV, 1))
    B.mark("maxq", sV)
    side_norms("k", Kfm, "kev_done", "qn2c7")
    B.wait("vector", "kn")
    B.op("vector", "reduce_max", gsc[0:16, 1:2], qkn2[:, :], axis=AX.X)
    B.op("vector", "tensor_tensor", gsc[0:16, 2:3], gsc[0:16, 0:1],
         gsc[0:16, 1:2], op=OP.mult)
    B.op("vector", "tensor_scalar", gsc[0:16, 3:4], gsc[0:16, 2:3], -0.125,
         None, op0=OP.mult, inc=(sV, 1))
    B.mark("maxqk", sV)
    B.wait("tensor", "maxqk")
    B.bank_acquire("tensor", 4)
    B.op("tensor", "transpose", bank[4][0:1, 0:16], gsc[0:16, 3:4],
         identf[0:16, 0:16], inc=(sPE, 1))
    B.mark("maxkT", sPE)
    B.wait("vector", "maxkT")
    B.op("vector", "tensor_copy", maxk_row[:, :], bank[4][0:1, 0:16],
         inc=(sV, 1))
    B.mark("maxkrow", sV)
    B.bank_release(4, "maxkrow")
    B.wait("tensor", "maxkrow")
    B.bank_acquire("tensor", 5)
    B.op("tensor", "matmul", bank[5][:, 0:16], ones_f[:, :], maxk_row[:, :],
         start=True, stop=True, inc=(sPE, 1))
    B.mark("maxkbcmm", sPE)
    B.wait("vector", "maxkbcmm")
    B.op("vector", "tensor_copy", bias_hd[:, :], bank[5][:, 0:16],
         inc=(sV, 1))
    B.mark("bias_all", sV)
    B.bank_release(5, "bias_all")

    if upto == "norms":
        _finish_debug()
        return _playback_and_close()
    B.wait("scalar", "bias_all")
    for pp in range(4):
        pairs = (2 * pp, 2 * pp + 1)
        for qg in range(4):
            q0 = qg * 512
            for ps in range(2):
                for half in range(2):
                    i = 4 + 2 * ps + half
                    B.bank_acquire("tensor", i)
                    B.op("tensor", "matmul", bank[i][:, :], zerosb[:, :],
                         Qfm[0][:, 0:512], start=True, stop=True,
                         skip_group_check=True,
                         inc=(sPE, 1))
                    B.mark(f"pz_{pp}_{qg}_{ps}_{half}", sPE)
            for kc in range(4 * qg + 4):
                qs = max(q0, kc * 128)
                w = q0 + 512 - qs
                diag = kc >= 4 * qg
                for ps in range(2):
                    hp = pairs[ps]
                    for hh in range(2):
                        i = 2 * ps + hh
                        B.bank_acquire("tensor", i)
                        B.op("tensor", "matmul", bank[i][:, 0:w],
                             Kfm[hp][hh * 64:(hh + 1) * 64,
                                     kc * 128:(kc + 1) * 128],
                             Qfm[hp][hh * 64:(hh + 1) * 64, qs:qs + w],
                             start=True, stop=not diag,
                             tile_position=(hh * 64, 0),
                             inc=None if diag else (sPE, 1))
                        if diag:
                            B.op("tensor", "matmul", bank[i][:, 0:128],
                                 identb[:, :], dmaskb[:, :],
                                 start=False, stop=True, inc=(sPE, 1))
                        B.mark(f"sc_{pp}_{qg}_{kc}_{ps}_{hh}", sPE)
                        pr = kc % 2
                        B.wait("scalar", f"sc_{pp}_{qg}_{kc}_{ps}_{hh}")
                        if kc >= 2:
                            B.wait("scalar", f"pv_{pp}_{qg}_{kc - 2}_{ps}_{hh}")
                        elif qg > 0:
                            B.wait("scalar",
                                   f"pv_{pp}_{qg - 1}_{4 * qg - 2 + kc}_{ps}_{hh}")
                        elif pp > 0:
                            B.wait("scalar", f"pv_{pp - 1}_3_{14 + kc}_{ps}_{hh}")
                        B.op("scalar", "activation",
                             piece[ps][hh][pr][:, 0:w], bank[i][:, 0:w],
                             AF.Exp,
                             bias=bias_hd[:, 2 * hp + hh:2 * hp + hh + 1],
                             scale=0.125, inc=(sA, 1))
                        B.mark(f"exp_{pp}_{qg}_{kc}_{ps}_{hh}", sA)
                        B.bank_release(i, f"exp_{pp}_{qg}_{kc}_{ps}_{hh}")
                        B.wait("tensor", f"exp_{pp}_{qg}_{kc}_{ps}_{hh}")
                        if kc == 0 and qg == 0 and pp == 0 and hh == 0 \
                                and ps == 0:
                            B.wait("tensor", "vones")
                        for qt in range(max(kc, 4 * qg), 4 * qg + 4):
                            j = qt - 4 * qg
                            i2 = 4 + 2 * ps + j // 2
                            co = (j % 2) * 130 + hh * 65
                            B.op("tensor", "matmul",
                                 bank[i2][:, co:co + 65],
                                 piece[ps][hh][pr][:,
                                     qt * 128 - qs:qt * 128 - qs + 128],
                                 Vtm[kc][:, (2 * hp + hh) * 65:
                                         (2 * hp + hh) * 65 + 65],
                                 start=False, stop=True,
                                 skip_group_check=True,
                                 inc=(sPE, 1) if qt == 4 * qg + 3 or
                                     (kc >= 4 * qg and qt == 4 * qg + 3)
                                     else None)
                        B.mark(f"pv_{pp}_{qg}_{kc}_{ps}_{hh}", sPE)
            for ps in range(2):
                hp = pairs[ps]
                B.wait("vector", f"pv_{pp}_{qg}_{4 * qg + 3}_{ps}_1")
                if pp >= 1:
                    B.wait("vector", f"atT_{pp - 1}_{ps}")
                for j in range(4):
                    qt = 4 * qg + j
                    i2 = 4 + 2 * ps + j // 2
                    co = (j % 2) * 130
                    for hh in range(2):
                        B.op("vector", "reciprocal", rcsc[:, hh:hh + 1],
                             bank[i2][:, co + hh * 65 + 64:co + hh * 65 + 65])
                    for hh in range(2):
                        B.op("vector", "tensor_scalar",
                             attn_tm[ps][:, qt * 128 + hh * 64:
                                         qt * 128 + hh * 64 + 64],
                             bank[i2][:, co + hh * 65:co + hh * 65 + 64],
                             rcsc[:, hh:hh + 1], None, op0=OP.mult,
                             inc=(sV, 1) if (j % 2 == 1 and hh == 1) else None)
                    if j % 2 == 1:
                        B.mark(f"pvev_{pp}_{qg}_{ps}_{j // 2}", sV)
                        B.bank_release(i2, f"pvev_{pp}_{qg}_{ps}_{j // 2}")
        for ps in range(2):
            hp = pairs[ps]
            B.wait("tensor", f"pvev_{pp}_3_{ps}_1")
            for qt in range(NTT):
                i = 2 * ps + (qt // 4) % 2
                if qt % 4 == 0:
                    B.bank_acquire("tensor", i)
                B.op("tensor", "transpose",
                     bb(i)[:, (qt % 4) * 128:(qt % 4) * 128 + 128],
                     attn_tm[ps][:, qt * 128:(qt + 1) * 128], identb[:, :],
                     inc=(sPE, 1) if qt % 4 == 3 else None)
                if qt % 4 == 3:
                    B.mark(f"atTm_{pp}_{ps}_{qt // 4}", sPE)
                    B.wait("vector", f"atTm_{pp}_{ps}_{qt // 4}")
                    B.op("vector", "tensor_copy",
                         afm[hp][:, (qt - 3) * 128:(qt + 1) * 128],
                         bb(i)[:, 0:512], inc=(sV, 1))
                    B.mark(f"afmev_{pp}_{ps}_{qt // 4}", sV)
                    B.bank_release(i, f"afmev_{pp}_{ps}_{qt // 4}")
            B.op("vector", "nop", inc=(sV, 1))
            B.mark(f"atT_{pp}_{ps}", sV)
    B.mark("attn_done", sV)

    if upto == "attn":
        _finish_debug()
        return _playback_and_close()
    B.wait("tensor", "attn_done")
    B.wait("tensor", "wo")
    B.wait("sync", "kn")
    for tt in range(NTT):
        if tt >= 2:
            B.wait("sync", f"h1_{tt - 2}")
        s_dma(xstage[tt % 2][:, :], dx[tt * 128:(tt + 1) * 128, :],
              ev=f"xf32_{tt}")
    bi = 0
    for tt in range(NTT):
        for dog in range(2):
            i = bi % 4
            bk = bank[i]
            B.bank_acquire("tensor", i)
            for dc in range(NCK):
                B.op("tensor", "matmul", bk[:, :],
                     afm[dc][:, tt * 128:(tt + 1) * 128],
                     wslot[1][:, dc, dog * 512:(dog + 1) * 512],
                     start=(dc == 0), stop=False)
            B.op("tensor", "matmul", bk[:, :], ones_b[:, :],
                 outb_row[0:1, dog * 512:(dog + 1) * 512],
                 start=False, stop=True, inc=(sPE, 1))
            B.mark(f"omm{bi}", sPE)
            B.wait("vector", f"omm{bi}")
            if dog == 0:
                B.wait("vector", f"xf32_{tt}")
            if dbg_nores:
                B.op("vector", "tensor_scalar",
                     x1_tm[tt][:, dog * 512:(dog + 1) * 512], bk[:, :],
                     1.0, 0.0, op0=OP.mult, op1=OP.add,
                     accum_out=lnst[:, dog:dog + 1], inc=(sV, 1))
            else:
                B.op("vector", "scalar_tensor_tensor",
                     x1_tm[tt][:, dog * 512:(dog + 1) * 512], bk[:, :], 1.0,
                     xstage[tt % 2][:, dog * 512:(dog + 1) * 512],
                     op0=OP.mult, op1=OP.add,
                     accum_out=lnst[:, dog:dog + 1],
                     inc=(sV, 1))
            B.mark(f"h1ev{bi}", sV)
            B.bank_release(i, f"h1ev{bi}")
            bi += 1
        B.mark(f"h1_{tt}", sV)
        if dbg_h1 and tt < 2:
            dh = nc.dram_tensor(f"dbg_h1_{tt}", [128, D], F32,
                                kind="ExternalOutput").ap()
            dxs = nc.dram_tensor(f"dbg_xs_{tt}", [128, D], F32,
                                 kind="ExternalOutput").ap()
            B.wait("scalar", f"h1_{tt}")
            B.op("scalar", "dma_start", out=dh, in_=x1_tm[tt][:, :],
                 inc=(sDs, 16))
            B.op("scalar", "dma_start", out=dxs, in_=xstage[tt % 2][:, :],
                 inc=(sDs, 16))
            B.wait_val("vector", sDs, sDs.n)
        B.wait("scalar", f"h1_{tt}")
        B.op("scalar", "activation", x1bfst[tt % 2][:, :], x1_tm[tt][:, :],
             AF.Square, accum_out=lnst[:, 2:3], inc=(sA, 1))
        B.mark(f"sq1_{tt}", sA)
        B.wait("vector", f"sq1_{tt}")
        B.op("vector", "tensor_reduce", lnst[:, 3:4], lnst[:, 0:2], axis=AX.X,
             op=OP.add)
        B.op("vector", "tensor_scalar", lnst[:, 3:4], lnst[:, 3:4], 1.0 / D,
             None, op0=OP.mult)
        B.op("vector", "tensor_scalar", lnst[:, 4:5], lnst[:, 3:4],
             lnst[:, 3:4], None, op0=OP.mult)
        B.op("vector", "scalar_tensor_tensor", lnst[:, 5:6], lnst[:, 2:3],
             1.0 / D, lnst[:, 4:5], op0=OP.mult, op1=OP.subtract)
        B.op("vector", "tensor_scalar", lnst[:, 5:6], lnst[:, 5:6], LN_EPS,
             None, op0=OP.add)
        B.op("vector", "reciprocal", lnst[:, 6:7], lnst[:, 5:6], inc=(sV, 1))
        B.mark(f"st1a_{tt}", sV)
        B.wait("scalar", f"st1a_{tt}")
        B.op("scalar", "activation", lnst[:, 7:8], lnst[:, 6:7], AF.Sqrt,
             inc=(sA, 1))
        B.mark(f"st1b_{tt}", sA)
        B.wait("vector", f"st1b_{tt}")
        if not dbg_nores:
            B.op("vector", "tensor_scalar", x1_tm[tt][:, :], x1_tm[tt][:, :],
                 lnst[:, 3:4], lnst[:, 7:8], op0=OP.subtract, op1=OP.mult)
        B.op("vector", "tensor_copy", x1bfst[tt % 2][:, :], x1_tm[tt][:, :],
             inc=(sV, 1))
        B.mark(f"x1b_{tt}", sV)
        B.wait("tensor", f"x1b_{tt}")
        for c in range(NCK):
            i = 4 + c // 4
            if c % 4 == 0:
                B.bank_acquire("tensor", i)
            B.op("tensor", "transpose",
                 bb(i)[:, (c % 4) * 128:(c % 4) * 128 + 128],
                 x1bfst[tt % 2][:, c * 128:(c + 1) * 128], identb[:, :],
                 inc=(sPE, 1) if c == NCK - 1 else None)
        B.mark(f"x1T_{tt}", sPE)
        B.wait("vector", f"x1T_{tt}")
        for half in range(2):
            i = 4 + half
            B.op("vector", "tensor_copy",
                 A_v[:, 0:16384].rearrange("p (c t) -> p c t", c=8)
                 [:, half * 4:half * 4 + 4, tt * 128:(tt + 1) * 128],
                 bb(i)[:, 0:512].rearrange("p (c t) -> p c t", c=4),
                 inc=(sV, 1))
            B.mark(f"x1fev_{tt}_{half}", sV)
            B.bank_release(i, f"x1fev_{tt}_{half}")
    B.mark("omm_done", sPE)

    if upto == "ln1":
        _finish_debug()
        return _playback_and_close()
    B.wait("tensor", "smallw")
    for tt in range(NTT):
        B.wait("tensor", f"x1fev_{tt}_1")
        i = 6
        B.bank_acquire("tensor", i)
        for dc in range(NCK):
            B.op("tensor", "matmul", bank[i][:, 0:NE],
                 x1fm[dc][:, tt * 128:(tt + 1) * 128],
                 gw_sb[:, dc * NE:(dc + 1) * NE],
                 start=(dc == 0), stop=False)
        B.op("tensor", "matmul", bank[i][:, 0:NE], ones_b[:, :],
             gb_row[0:1, :], start=False, stop=True, inc=(sPE, 1))
        B.mark(f"gmm_{tt}", sPE)
        B.wait("vector", f"gmm_{tt}")
        sc = scores_g[:, tt * NE:(tt + 1) * NE]
        B.op("vector", "tensor_scalar", sc, bank[i][:, 0:NE], EPS_GATE, None,
             op0=OP.max, inc=(sV, 1))
        B.mark(f"gev_{tt}", sV)
        B.bank_release(i, f"gev_{tt}")
        B.op("vector", "reduce_max", gsc[:, 0:1], sc, axis=AX.X, negate=True,
             inc=(sV, 1))
        B.mark(f"gmx_{tt}", sV)
        B.wait("scalar", f"gmx_{tt}")
        B.op("scalar", "activation", p_g[:, :], sc, AF.Exp,
             bias=gsc[:, 0:1], scale=1.0, accum_out=gsc[:, 1:2], inc=(sA, 1))
        B.mark(f"gexp_{tt}", sA)
        B.wait("vector", f"gexp_{tt}")
        B.op("vector", "reciprocal", gsc[:, 2:3], gsc[:, 1:2])
        B.op("vector", "tensor_scalar", p_g[:, :], p_g[:, :], gsc[:, 2:3],
             None, op0=OP.mult)
        B.op("vector", "reduce_max", gsc[:, 3:4], p_g[:, :], axis=AX.X,
             negate=True)
        B.op("vector", "tensor_scalar", g8a[:, :], p_g[:, :], gsc[:, 3:4],
             None, op0=OP.add)
        B.op("vector", "tensor_single_scalar", g8a[:, :], g8a[:, :], 0.0,
             OP.is_ge)
        B.op("vector", "tensor_scalar", g8a[:, :], g8a[:, :], -BIG, BIG,
             op0=OP.mult, op1=OP.add)
        B.op("vector", "tensor_tensor", g8a[:, :], g8a[:, :], idx8[:, :],
             op=OP.add)
        B.op("vector", "tensor_reduce", gsc[:, 4:5], g8a[:, :], axis=AX.X,
             op=OP.min, negate=True)
        B.op("vector", "tensor_scalar", g8a[:, :], idx8[:, :], gsc[:, 4:5],
             None, op0=OP.add)
        B.op("vector", "tensor_single_scalar", g8a[:, :], g8a[:, :], 0.0,
             OP.is_equal)
        B.op("vector", "scalar_tensor_tensor", g8b[:, :], g8a[:, :], -BIG,
             p_g[:, :], op0=OP.mult, op1=OP.add)
        B.op("vector", "reduce_max", gsc[:, 5:6], g8b[:, :], axis=AX.X,
             negate=True)
        B.op("vector", "tensor_scalar", g8b[:, :], g8b[:, :], gsc[:, 5:6],
             None, op0=OP.add)
        B.op("vector", "tensor_single_scalar", g8b[:, :], g8b[:, :], 0.0,
             OP.is_ge)
        B.op("vector", "tensor_scalar", g8b[:, :], g8b[:, :], -BIG, BIG,
             op0=OP.mult, op1=OP.add)
        B.op("vector", "tensor_tensor", g8b[:, :], g8b[:, :], idx8[:, :],
             op=OP.add)
        B.op("vector", "tensor_reduce", gsc[:, 6:7], g8b[:, :], axis=AX.X,
             op=OP.min, negate=True)
        B.op("vector", "tensor_scalar", g8b[:, :], idx8[:, :], gsc[:, 6:7],
             None, op0=OP.add)
        B.op("vector", "tensor_single_scalar", g8b[:, :], g8b[:, :], 0.0,
             OP.is_equal)
        B.op("vector", "tensor_tensor", g8a[:, :], g8a[:, :], g8b[:, :],
             op=OP.add)
        cb = comb_g[:, tt * NE:(tt + 1) * NE]
        B.op("vector", "tensor_scalar", cb, p_g[:, :], W_THRES, None,
             op0=OP.max)
        B.op("vector", "tensor_tensor", cb, cb, g8a[:, :], op=OP.mult)
        B.op("vector", "tensor_reduce", gsc[:, 7:8], cb, axis=AX.X, op=OP.add)
        B.op("vector", "reciprocal", gsc[:, 8:9], gsc[:, 7:8])
        B.op("vector", "tensor_scalar", cb, cb, gsc[:, 8:9], 0.5,
             op0=OP.mult, op1=OP.mult)
        B.op("vector", "tensor_copy", combbf[:, :], cb, inc=(sV, 1))
        B.mark(f"comb_{tt}", sV)
        B.wait("tensor", f"comb_{tt}")
        i = 6 + tt % 2
        B.bank_acquire("tensor", i)
        B.op("tensor", "transpose", bb(i)[0:NE, 0:128], combbf[:, :],
             identb[:, :], inc=(sPE, 1))
        B.mark(f"combT_{tt}", sPE)
        B.wait("vector", f"combT_{tt}")
        B.op("vector", "tensor_copy", combT[:, tt * 128:(tt + 1) * 128],
             bb(i)[0:NE, 0:128], inc=(sV, 1))
        B.mark(f"combTev_{tt}", sV)
        B.bank_release(i, f"combTev_{tt}")

    if upto == "gate":
        _finish_debug()
        return _playback_and_close()
    B.wait("tensor", "smallw")
    bi = 0
    for tt in range(NTT):
        B.wait("tensor", f"combTev_{tt}")
        if tt == 14:
            B.wait("vector", f"sq1_{NTT - 1}")
        for dog in range(2):
            i = bi % 4
            B.bank_acquire("tensor", i)
            B.op("tensor", "matmul", bank[i][:, :],
                 combT[:, tt * 128:(tt + 1) * 128],
                 eb_sb[:, dog * 512:(dog + 1) * 512],
                 start=True, stop=True, inc=(sPE, 1))
            B.mark(f"bmm{bi}", sPE)
            B.wait("vector", f"bmm{bi}")
            B.op("vector", "tensor_copy",
                 acc[tt][:, dog * 512:(dog + 1) * 512], bank[i][:, :],
                 inc=(sV, 1))
            B.mark(f"bev{bi}", sV)
            B.bank_release(i, f"bev{bi}")
            bi += 1

    bi = 0
    for e in range(NE):
        B.wait("tensor", f"we{e}")
        ws = wslot[e % 2]
        for tt in range(NTT):
            for dog in range(2):
                i = bi % 6
                B.bank_acquire("tensor", i)
                for dc in range(NCK):
                    B.op("tensor", "matmul", bank[i][:, :],
                         x1fm[dc][:, tt * 128:(tt + 1) * 128],
                         ws[:, dc, dog * 512:(dog + 1) * 512],
                         start=(dc == 0), stop=(dc == NCK - 1),
                         inc=(sPE, 1) if dc == NCK - 1 else None)
                B.mark(f"emm{bi}", sPE)
                B.wait("vector", f"emm{bi}")
                B.op("vector", "scalar_tensor_tensor",
                     acc[tt][:, dog * 512:(dog + 1) * 512], bank[i][:, :],
                     comb_g[:, tt * NE + e:tt * NE + e + 1],
                     acc[tt][:, dog * 512:(dog + 1) * 512],
                     op0=OP.mult, op1=OP.add, inc=(sV, 1))
                B.mark(f"eev{bi}", sV)
                B.bank_release(i, f"eev{bi}")
                bi += 1
        B.mark(f"expmm{e}", sPE)

    if upto == "ffn":
        _finish_debug()
        return _playback_and_close()
    for tt in range(NTT):
        B.op("vector", "scalar_tensor_tensor", acc[tt][:, 0:512],
             acc[tt][:, 0:512], 1.0, x1_tm[tt][:, 0:512],
             op0=OP.mult, op1=OP.add, accum_out=lnst[:, 0:1])
        B.op("vector", "scalar_tensor_tensor", acc[tt][:, 512:1024],
             acc[tt][:, 512:1024], 1.0, x1_tm[tt][:, 512:1024],
             op0=OP.mult, op1=OP.add, accum_out=lnst[:, 1:2], inc=(sV, 1))
        B.mark(f"h2_{tt}", sV)
        B.wait("scalar", f"h2_{tt}")
        if tt == 0:
            B.bank_acquire("scalar", 6)
            B.bank_acquire("scalar", 7)
        B.op("scalar", "activation", bank[6][:, :], acc[tt][:, 0:512],
             AF.Square, accum_out=lnst[:, 2:3])
        B.op("scalar", "activation", bank[7][:, :], acc[tt][:, 512:1024],
             AF.Square, accum_out=lnst[:, 4:5], inc=(sA, 1))
        B.mark(f"sq2_{tt}", sA)
        B.wait("vector", f"sq2_{tt}")
        B.op("vector", "tensor_reduce", lnst[:, 3:4], lnst[:, 0:2], axis=AX.X,
             op=OP.add)
        B.op("vector", "tensor_scalar", lnst[:, 3:4], lnst[:, 3:4], 1.0 / D,
             None, op0=OP.mult)
        B.op("vector", "tensor_tensor", lnst[:, 2:3], lnst[:, 2:3],
             lnst[:, 4:5], op=OP.add)
        B.op("vector", "tensor_scalar", lnst[:, 4:5], lnst[:, 3:4],
             lnst[:, 3:4], None, op0=OP.mult)
        B.op("vector", "scalar_tensor_tensor", lnst[:, 5:6], lnst[:, 2:3],
             1.0 / D, lnst[:, 4:5], op0=OP.mult, op1=OP.subtract)
        B.op("vector", "tensor_scalar", lnst[:, 5:6], lnst[:, 5:6], LN_EPS,
             None, op0=OP.add)
        B.op("vector", "reciprocal", lnst[:, 6:7], lnst[:, 5:6], inc=(sV, 1))
        B.mark(f"st2a_{tt}", sV)
        B.wait("scalar", f"st2a_{tt}")
        B.op("scalar", "activation", lnst[:, 7:8], lnst[:, 6:7], AF.Sqrt,
             inc=(sA, 1))
        B.mark(f"st2b_{tt}", sA)
        B.wait("vector", f"st2b_{tt}")
        B.op("vector", "tensor_scalar", acc[tt][:, :], acc[tt][:, :],
             lnst[:, 3:4], lnst[:, 7:8], op0=OP.subtract, op1=OP.mult,
             inc=(sV, 1))
        B.mark(f"out_{tt}", sV)
        B.wait("sync", f"out_{tt}")
        s_dma(dy[tt * 128:(tt + 1) * 128, :], acc[tt][:, :], ev=f"st_{tt}")
    B.wait_val("sync", sDs, sDs.n)

    return _playback_and_close()


_NC_CACHE = {}


def _get_nc():
    if "nc" not in _NC_CACHE:
        _NC_CACHE["nc"] = build_program()
    return _NC_CACHE["nc"]


def _consts():
    identf = np.eye(128, dtype=np.float32)
    identb = identf.astype(bf16)
    k = np.arange(128)[:, None]
    q = np.arange(128)[None, :]
    dmask = np.where(k > q, NEG, 0.0).astype(np.float32).astype(bf16)
    sel = np.zeros((128, 128), np.float32)
    for c in range(8):
        sel[0:64, c * 16 + 2 * c] = 1.0
        sel[64:128, c * 16 + 2 * c + 1] = 1.0
    selb = sel.astype(bf16)
    idx8 = np.broadcast_to(np.arange(8, dtype=np.float32), (128, 8)).copy()
    return identf, identb, dmask, selb, idx8


def make_in_maps(x, wq, wk, wv, wo, outb, gw, gb, ew, eb):
    identf, identb, dmask, selb, idx8 = _consts()
    in_maps = []
    for i in range(8):
        in_maps.append({
            "x": np.ascontiguousarray(x[i]),
            "wq": wq, "wk": wk, "wv": wv, "wo": wo, "outb": outb,
            "gw": gw, "gb": gb, "ew": ew, "eb": eb,
            "identf": identf, "identb": identb, "dmask": dmask,
            "sel": selb, "idx8": idx8,
        })
    return in_maps


def kernel(L0_attn_batch, L0_attn_w_query, L0_attn_w_key, L0_attn_w_value,
           L0_attn_out_w, L0_attn_out_b, L0_ffn_gate_W, L0_ffn_gate_b,
           L0_ffn_expert_W, L0_ffn_expert_b):
    x = np.ascontiguousarray(np.asarray(L0_attn_batch, dtype=np.float32))
    wq = np.ascontiguousarray(np.asarray(L0_attn_w_query, dtype=np.float32))
    wk = np.ascontiguousarray(np.asarray(L0_attn_w_key, dtype=np.float32))
    wv = np.ascontiguousarray(np.asarray(L0_attn_w_value, dtype=np.float32))
    wo = np.ascontiguousarray(np.asarray(L0_attn_out_w, dtype=np.float32))
    outb = np.ascontiguousarray(
        np.asarray(L0_attn_out_b, dtype=np.float32)).reshape(1, D)
    gw = np.ascontiguousarray(np.asarray(L0_ffn_gate_W, dtype=np.float32))
    gb = np.ascontiguousarray(
        np.asarray(L0_ffn_gate_b, dtype=np.float32)).reshape(1, NE)
    ew = np.ascontiguousarray(np.asarray(L0_ffn_expert_W, dtype=np.float32))
    eb = np.ascontiguousarray(np.asarray(L0_ffn_expert_b, dtype=np.float32))
    nc = _get_nc()
    in_maps = make_in_maps(x, wq, wk, wv, wo, outb, gw, gb, ew, eb)
    res = run_bass_kernel_spmd(nc, in_maps, core_ids=list(range(8)))
    out = np.stack([np.asarray(r["y"], dtype=np.float32)
                    for r in res.results], axis=0)
    return out


# version 03
# speedup vs baseline: 1.4878x; 1.3656x over previous
import numpy as np
import ml_dtypes
from contextlib import ExitStack
import concourse.bass as bass
import concourse.mybir as mybir
from concourse.bass_utils import run_bass_kernel_spmd

F32 = mybir.dt.float32
BF16 = mybir.dt.bfloat16
OP = mybir.AluOpType
AF = mybir.ActivationFunctionType
AX = mybir.AxisListType
bf16 = ml_dtypes.bfloat16

S = 2048
D = 1024
H = 16
HD = 64
NCK = 8
NTT = 16
NE = 8
EPS_GATE = 1e-4
W_THRES = 0.1
LN_EPS = 1e-5
NEG = -1e9
BIG = 1e9

ENGS = ("tensor", "vector", "scalar", "sync", "gpsimd")


class _Sem:
    def __init__(self, name):
        self.name = name
        self.n = 0
        self.h = None


def _ap_iv(x):
    if not (hasattr(x, "tensor") and hasattr(x, "offset") and hasattr(x, "ap")):
        return None
    try:
        esz = mybir.dt.size(x.dtype)
        dims = list(x.ap)
        ext = 1
        for step, count in dims[1:]:
            ext += abs(int(step)) * (int(count) - 1)
        lo = int(x.offset) * esz
        return (x.tensor.name, lo, lo + ext * esz)
    except Exception:
        return None


_HAZ_ENGS = ("vector", "scalar", "gpsimd")
_NO_TRACK = {"dma_start", "nop", "drain", "memzero"}
_WINDOW = 12


class Builder:
    def __init__(self, nc, lenient=False):
        self.nc = nc
        self.lenient = lenient
        self.code = {e: [] for e in ENGS}
        self.events = {}
        self.sems = {}
        self.bank_free = {}
        self.recent = {e: [] for e in _HAZ_ENGS}
        self.n_drains = 0

    def sem(self, name):
        s = _Sem(name)
        self.sems[name] = s
        return s

    def op(self, eng, meth, *a, inc=None, **kw):
        if eng in _HAZ_ENGS and meth not in _NO_TRACK:
            writes, reads = [], []
            for i, x in enumerate(a):
                iv = _ap_iv(x)
                if iv is None:
                    continue
                (writes if i == 0 else reads).append(iv)
            for k2, x in kw.items():
                iv = _ap_iv(x)
                if iv is None:
                    continue
                (writes if k2 in ("out", "accum_out") else reads).append(iv)
            if meth == "memset":
                pass
            hit = False
            for wr, nm, lo, hi in ([(1, *w) for w in writes] +
                                   [(0, *r) for r in reads]):
                for owr, onm, olo, ohi in self.recent[eng]:
                    if (wr or owr) and nm == onm and lo < ohi and olo < hi:
                        hit = True
                        break
                if hit:
                    break
            if hit:
                self.code[eng].append(("op", "drain", (), {}, None))
                self.recent[eng] = []
                self.n_drains += 1
            self.recent[eng].extend([(1, *w) for w in writes] +
                                    [(0, *r) for r in reads])
            self.recent[eng] = self.recent[eng][-_WINDOW:]
        if inc is not None:
            sem, k = inc
            sem.n += k
        self.code[eng].append(("op", meth, a, kw, inc))

    def wait(self, eng, ev):
        assert isinstance(ev, str)
        self.code[eng].append(("wait", ev))

    def wait_val(self, eng, sem, val):
        self.code[eng].append(("waitv", sem, val))

    def mark(self, ev, sem):
        assert ev not in self.events, ev
        self.events[ev] = (sem, sem.n)

    def bank_acquire(self, eng, idx):
        ev = self.bank_free.get(idx)
        if ev is not None:
            self.wait(eng, ev)

    def bank_release(self, idx, ev):
        self.bank_free[idx] = ev

    def playback(self, eng_name, eng):
        for item in self.code[eng_name]:
            if item[0] == "op":
                _, meth, a, kw, inc = item
                inst = getattr(eng, meth)(*a, **kw)
                if inc is not None:
                    sem, k = inc
                    inst.then_inc(sem.h, k)
            elif item[0] == "wait":
                if item[1] not in self.events:
                    assert self.lenient, item[1]
                    continue
                sem, val = self.events[item[1]]
                if val > 0:
                    eng.wait_ge(sem.h, val)
            else:
                _, sem, val = item
                if val > 0:
                    eng.wait_ge(sem.h, val)


def ceil_div(a, b):
    return (a + b - 1) // b


def build_program(upto=None, dumps=(), dbg_h1=False, dbg_nores=False):
    nc = bass.Bass("TRN2", target_bir_lowering=False)
    B = Builder(nc, lenient=upto is not None)

    dx = nc.dram_tensor("x", [S, D], F32, kind="ExternalInput").ap()
    dwq = nc.dram_tensor("wq", [D, D], F32, kind="ExternalInput").ap()
    dwk = nc.dram_tensor("wk", [D, D], F32, kind="ExternalInput").ap()
    dwv = nc.dram_tensor("wv", [D, D], F32, kind="ExternalInput").ap()
    dwo = nc.dram_tensor("wo", [D, D], F32, kind="ExternalInput").ap()
    doutb = nc.dram_tensor("outb", [1, D], F32, kind="ExternalInput").ap()
    dgw = nc.dram_tensor("gw", [D, NE], F32, kind="ExternalInput").ap()
    dgb = nc.dram_tensor("gb", [1, NE], F32, kind="ExternalInput").ap()
    dew = nc.dram_tensor("ew", [NE, D, D], F32, kind="ExternalInput").ap()
    deb = nc.dram_tensor("eb", [NE, D], F32, kind="ExternalInput").ap()
    didf = nc.dram_tensor("identf", [128, 128], F32, kind="ExternalInput").ap()
    didb = nc.dram_tensor("identb", [128, 128], BF16, kind="ExternalInput").ap()
    dmask = nc.dram_tensor("dmask", [128, 128], BF16, kind="ExternalInput").ap()
    dsel = nc.dram_tensor("sel", [128, 128], BF16, kind="ExternalInput").ap()
    didx = nc.dram_tensor("idx8", [128, NE], F32, kind="ExternalInput").ap()
    dy = nc.dram_tensor("y", [S, D], F32, kind="ExternalOutput").ap()

    ctx = ExitStack()
    SB = lambda name, shape, dt: ctx.enter_context(nc.sbuf_tensor(name, shape, dt))

    A_w = SB("A_w", [128, 16384], BF16)
    A_qk = SB("A_qk", [128, 32768], BF16)
    A_v = SB("A_v", [128, 16640], BF16)
    A_xa = SB("A_xa", [128, 32768], BF16)

    def carve(arena, byte_off, nparts, dt, nfree):
        if dt == BF16:
            assert byte_off % 2 == 0
            return arena[0:nparts, byte_off // 2: byte_off // 2 + nfree]
        assert byte_off % 4 == 0
        return arena[:, :].bitcast(F32)[0:nparts, byte_off // 4: byte_off // 4 + nfree]

    wslot = [carve(A_w, i * 16384, 128, BF16, 8192).rearrange(
        "p (c o) -> p c o", c=NCK) for i in range(2)]
    Qfm = [carve(A_qk, c * 4096, 128, BF16, 2048) for c in range(NCK)]
    Kfm = [carve(A_qk, 32768 + c * 4096, 128, BF16, 2048) for c in range(NCK)]
    x1_tm = [carve(A_qk, tt * 4096, 128, F32, 1024) for tt in range(NTT)]
    Vtm = [carve(A_v, tt * 2080, 128, BF16, 1040) for tt in range(NTT)]
    xbstage = [carve(A_v, i * 2048, 128, BF16, 1024) for i in range(2)]
    x1fm = [carve(A_v, c * 4096, 128, BF16, 2048) for c in range(NCK)]
    xfm = [carve(A_xa, c * 4096, 128, BF16, 2048) for c in range(NCK)]
    afm = xfm
    WK = 32768
    piece = [[[carve(A_xa, WK + (((ps * 2 + hh) * 2 + pr) * 1024), 128, BF16,
                     512) for pr in range(2)] for hh in range(2)]
             for ps in range(2)]
    attn_tm = [carve(A_xa, WK + 8192 + ps * 4096, 128, BF16, 2048)
               for ps in range(2)]
    qsq = carve(A_xa, WK + 16384, 128, BF16, 2048)
    qkn2 = carve(A_xa, WK + 20480, 16, F32, 2048)
    xstage = [carve(A_xa, WK + 16384 + i * 4096, 128, F32, 1024)
              for i in range(2)]
    x1bfst = [carve(A_xa, WK + 24576 + i * 2048, 128, BF16, 1024)
              for i in range(2)]
    acc = [carve(A_xa, tt * 4096, 128, F32, 1024) for tt in range(NTT)]

    identf = SB("identf_sb", [128, 128], F32)
    identb = SB("identb_sb", [128, 128], BF16)
    dmaskb = SB("dmask_sb", [128, 128], BF16)
    selb = SB("sel_sb", [128, 128], BF16)
    idx8 = SB("idx8_sb", [128, NE], F32)
    ones_f = SB("ones_f", [1, 128], F32)
    ones_b = SB("ones_b", [1, 128], BF16)
    outb_row = SB("outb_row", [1, D], BF16)
    gw_sb = SB("gw_sb", [128, NCK * NE], BF16)
    gb_row = SB("gb_row", [1, NE], BF16)
    eb_sb = SB("eb_sb", [NE, D], BF16)
    bias_hd = SB("bias_hd", [128, 16], F32)
    maxk_row = SB("maxk_row", [1, 16], F32)
    zerosb = SB("zerosb", [128, 128], BF16)
    rcsc = SB("rcsc", [128, 4], F32)
    combT = SB("combT", [NE, S], BF16)
    scores_g = SB("scores_g", [128, NTT * NE], F32)
    p_g = SB("p_g", [128, NE], F32)
    comb_g = SB("comb_g", [128, NTT * NE], F32)
    gsc = SB("gsc", [128, 16], F32)
    g8a = SB("g8a", [128, NE], F32)
    g8b = SB("g8b", [128, NE], F32)
    lnst = SB("lnst", [128, 8], F32)
    combbf = SB("combbf", [128, NTT * NE], BF16)
    dbgst = SB("dbgst", [128, 128], F32)

    bank = [ctx.enter_context(nc.psum_tensor(f"bank{i}", [128, 512], F32))
            for i in range(8)]

    def bb(i):
        return bank[i][:, :].bitcast(BF16)

    sDg = B.sem("sDg")
    sDs = B.sem("sDs")
    sPE = B.sem("sPE")
    sV = B.sem("sV")
    sA = B.sem("sA")

    def g_dma(out, in_, ev=None):
        B.op("gpsimd", "dma_start", out=out, in_=in_, inc=(sDg, 16))
        if ev:
            B.mark(ev, sDg)

    def s_dma(out, in_, ev=None):
        B.op("sync", "dma_start", out=out, in_=in_, inc=(sDs, 16))
        if ev:
            B.mark(ev, sDs)


    def _finish_debug():
        B.wait_val("sync", sPE, sPE.n)
        B.wait_val("sync", sV, sV.n)
        B.wait_val("sync", sA, sA.n)
        B.wait_val("sync", sDg, sDg.n)
        dump_views = {
            "xfm": xfm, "Qfm": Qfm, "Kfm": Kfm, "Vtm": Vtm, "afm": afm,
            "x1_tm": x1_tm, "x1fm": x1fm, "acc": acc,
            "bias_all": [bias_hd[:, :]],
            "comb_g": [comb_g[:, :]], "combT": [combT[:, :]],
            "scores_g": [scores_g[:, :]], "dbgst": [dbgst[:, :]],
        }
        for dn in dumps:
            views = dump_views[dn]
            for vi, ap in enumerate(views):
                shp = list(ap.shape)
                dt = ap.dtype
                dout = nc.dram_tensor(f"dump_{dn}_{vi}", shp, dt,
                                      kind="ExternalOutput").ap()
                s_dma(dout, ap)
        B.wait_val("sync", sDs, sDs.n)

    def _playback_and_close():
        for s in B.sems.values():
            s.h = ctx.enter_context(nc.semaphore(s.name))
        with nc.Block() as block:
            @block.tensor
            def _(t):
                B.playback("tensor", t)

            @block.vector
            def _(v):
                B.playback("vector", v)

            @block.scalar
            def _(s):
                B.playback("scalar", s)

            @block.sync
            def _(s):
                B.playback("sync", s)

            @block.gpsimd
            def _(g):
                B.playback("gpsimd", g)
        ctx.close()
        return nc

    s_dma(identf[:, :], didf)
    s_dma(identb[:, :], didb)
    s_dma(dmaskb[:, :], dmask)
    s_dma(selb[:, :], dsel)
    s_dma(idx8[:, :], didx, ev="consts")
    B.op("gpsimd", "memset", ones_f[:, :], 1.0)
    B.op("gpsimd", "memset", zerosb[:, :], 0.0)
    B.op("gpsimd", "memset", ones_b[:, :], 1.0, inc=(sDg, 1))
    B.mark("ones", sDg)

    for tt in range(NTT):
        if tt >= 2:
            B.wait("gpsimd", f"xtr{tt - 2}")
        g_dma(xbstage[tt % 2][:, :], dx[tt * 128:(tt + 1) * 128, :],
              ev=f"xld{tt}")
    g_dma(wslot[0][:, :, :], dwq.rearrange("(c p) o -> p c o", p=128), ev="wq")
    g_dma(wslot[1][:, :, :], dwk.rearrange("(c p) o -> p c o", p=128), ev="wk")
    g_dma(gw_sb[:, :].rearrange("p (c e) -> p c e", c=NCK),
          dgw.rearrange("(c p) e -> p c e", p=128))
    g_dma(gb_row[:, :], dgb)
    g_dma(outb_row[:, :], doutb)
    g_dma(eb_sb[:, :], deb, ev="smallw")
    B.wait("gpsimd", "qmm_done")
    g_dma(wslot[0][:, :, :], dwv.rearrange("(c p) o -> p c o", p=128), ev="wv")
    B.wait("gpsimd", "kmm_done")
    g_dma(wslot[1][:, :, :], dwo.rearrange("(c p) o -> p c o", p=128), ev="wo")
    B.wait("gpsimd", f"xtr{NTT - 1}")
    B.op("gpsimd", "memset",
         A_v[:, :].rearrange("p (t x) -> p t x", t=NTT)[:, :, 64:1040:65],
         1.0, inc=(sDg, 1))
    B.mark("vones", sDg)
    for e in range(NE):
        if e == 0:
            B.wait("gpsimd", "vmm_done")
        elif e == 1:
            B.wait("gpsimd", "omm_done")
        else:
            B.wait("gpsimd", f"expmm{e - 2}")
        g_dma(wslot[e % 2][:, :, :],
              dew[e].rearrange("(c p) o -> p c o", p=128), ev=f"we{e}")

    B.wait("tensor", "ones")
    B.wait("tensor", "consts")
    for tt in range(NTT):
        B.wait("tensor", f"xld{tt}")
        for c in range(NCK):
            i = 6 + c // 4
            if c % 4 == 0:
                B.bank_acquire("tensor", i)
            B.op("tensor", "transpose", bb(i)[:, (c % 4) * 128:(c % 4) * 128 + 128],
                 xbstage[tt % 2][:, c * 128:(c + 1) * 128], identb[:, :],
                 inc=(sPE, 1) if c == NCK - 1 else None)
        B.mark(f"xtr{tt}", sPE)
        B.wait("vector", f"xtr{tt}")
        for half in range(2):
            i = 6 + half
            B.op("vector", "tensor_copy",
                 A_xa[:, :].rearrange("p (c t) -> p c t", c=16)
                 [:, half * 4:half * 4 + 4, tt * 128:(tt + 1) * 128],
                 bb(i)[:, 0:512].rearrange("p (c t) -> p c t", c=4),
                 inc=(sV, 1))
            B.mark(f"xev{tt}_{half}", sV)
            B.bank_release(i, f"xev{tt}_{half}")

    if upto == "xfm":
        _finish_debug()
        return _playback_and_close()
    def proj_fm(name, slot_i, dst, w_ev):
        B.wait("tensor", w_ev)
        bi = 0
        for oc in range(NCK):
            for tg in range(4):
                bk = bank[bi % 4]
                B.bank_acquire("tensor", bi % 4)
                for dc in range(NCK):
                    B.op("tensor", "matmul", bk[:, :],
                         wslot[slot_i][:, dc, oc * 128:(oc + 1) * 128],
                         xfm[dc][:, tg * 512:(tg + 1) * 512],
                         start=(dc == 0), stop=(dc == NCK - 1),
                         inc=(sPE, 1) if dc == NCK - 1 else None)
                B.mark(f"{name}mm{bi}", sPE)
                B.wait("vector", f"{name}mm{bi}")
                B.op("vector", "tensor_copy",
                     dst[oc][:, tg * 512:(tg + 1) * 512], bk[:, :], inc=(sV, 1))
                B.mark(f"{name}ev{bi}", sV)
                B.bank_release(bi % 4, f"{name}ev{bi}")
                bi += 1
        B.mark(f"{name}mm_done", sPE)
        B.mark(f"{name}ev_done", sV)

    B.wait("tensor", f"xev{NTT - 1}_1")
    proj_fm("q", 0, Qfm, "wq")
    proj_fm("k", 1, Kfm, "wk")

    B.wait("tensor", "wv")
    bi = 0
    for tt in range(NTT):
        for dog in range(2):
            bk = bank[bi % 4]
            B.bank_acquire("tensor", bi % 4)
            for dc in range(NCK):
                B.op("tensor", "matmul", bk[:, :],
                     xfm[dc][:, tt * 128:(tt + 1) * 128],
                     wslot[0][:, dc, dog * 512:(dog + 1) * 512],
                     start=(dc == 0), stop=(dc == NCK - 1),
                     inc=(sPE, 1) if dc == NCK - 1 else None)
            B.mark(f"vmm{bi}", sPE)
            B.wait("vector", f"vmm{bi}")
            B.op("vector", "tensor_copy",
                 Vtm[tt][:, dog * 8 * 65:dog * 8 * 65 + 520]
                 .rearrange("p (h d) -> p h d", h=8)[:, :, 0:64],
                 bk[:, :].rearrange("p (h d) -> p h d", h=8), inc=(sV, 1))
            B.mark(f"vev{bi}", sV)
            B.bank_release(bi % 4, f"vev{bi}")
            bi += 1
    B.mark("vmm_done", sPE)

    if upto == "qkv":
        _finish_debug()
        return _playback_and_close()
    def side_norms(name, src, src_ev_v, prev_pe_ev):
        B.wait("scalar", src_ev_v)
        if prev_pe_ev is not None:
            B.wait("scalar", prev_pe_ev)
        for c in range(NCK):
            B.op("scalar", "activation", qsq[:, :], src[c][:, :], AF.Square,
                 inc=(sA, 1))
            B.mark(f"{name}sq{c}", sA)
            if c + 1 < NCK:
                B.wait("scalar", f"{name}n2c{c}")
            B.wait("tensor", f"{name}sq{c}")
            for tg in range(4):
                if c == 0:
                    B.bank_acquire("tensor", 4 + tg)
                B.op("tensor", "matmul", bank[4 + tg][0:16, :],
                     selb[:, c * 16:(c + 1) * 16],
                     qsq[:, tg * 512:(tg + 1) * 512],
                     start=(c == 0), stop=(c == NCK - 1),
                     inc=(sPE, 1) if tg == 3 else None)
            B.mark(f"{name}n2c{c}", sPE)
        B.wait("vector", f"{name}n2c{NCK - 1}")
        for tg in range(4):
            B.op("vector", "tensor_copy", qkn2[:, tg * 512:(tg + 1) * 512],
                 bank[4 + tg][0:16, :], inc=(sV, 1))
            B.mark(f"{name}n2e{tg}", sV)
            B.bank_release(4 + tg, f"{name}n2e{tg}")
        B.wait("scalar", f"{name}n2e3")
        B.op("scalar", "activation", qkn2[:, :], qkn2[:, :], AF.Sqrt,
             inc=(sA, 1))
        B.mark(f"{name}n", sA)

    side_norms("q", Qfm, "qev_done", None)
    B.wait("vector", "qn")
    B.op("vector", "reduce_max", gsc[0:16, 0:1], qkn2[:, :], axis=AX.X,
         inc=(sV, 1))
    B.mark("maxq", sV)
    side_norms("k", Kfm, "kev_done", "qn2c7")
    B.wait("vector", "kn")
    B.op("vector", "reduce_max", gsc[0:16, 1:2], qkn2[:, :], axis=AX.X)
    B.op("vector", "tensor_tensor", gsc[0:16, 2:3], gsc[0:16, 0:1],
         gsc[0:16, 1:2], op=OP.mult)
    B.op("vector", "tensor_scalar", gsc[0:16, 3:4], gsc[0:16, 2:3], -0.125,
         None, op0=OP.mult, inc=(sV, 1))
    B.mark("maxqk", sV)
    B.wait("tensor", "maxqk")
    B.bank_acquire("tensor", 4)
    B.op("tensor", "transpose", bank[4][0:1, 0:16], gsc[0:16, 3:4],
         identf[0:16, 0:16], inc=(sPE, 1))
    B.mark("maxkT", sPE)
    B.wait("vector", "maxkT")
    B.op("vector", "tensor_copy", maxk_row[:, :], bank[4][0:1, 0:16],
         inc=(sV, 1))
    B.mark("maxkrow", sV)
    B.bank_release(4, "maxkrow")
    B.wait("tensor", "maxkrow")
    B.bank_acquire("tensor", 5)
    B.op("tensor", "matmul", bank[5][:, 0:16], ones_f[:, :], maxk_row[:, :],
         start=True, stop=True, inc=(sPE, 1))
    B.mark("maxkbcmm", sPE)
    B.wait("vector", "maxkbcmm")
    B.op("vector", "tensor_copy", bias_hd[:, :], bank[5][:, 0:16],
         inc=(sV, 1))
    B.mark("bias_all", sV)
    B.bank_release(5, "bias_all")

    if upto == "norms":
        _finish_debug()
        return _playback_and_close()
    B.wait("scalar", "bias_all")

    def pv_emit(hp, qg, kc):
        for hh in range(2):
            B.wait("tensor", f"exp_{hp}_{qg}_{kc}_{hh}")
            if hp == 0 and qg == 0 and kc == 0 and hh == 0:
                B.wait("tensor", "vones")
            qs = max(qg * 512, kc * 128)
            for qt in range(max(kc, 4 * qg), 4 * qg + 4):
                j = qt - 4 * qg
                co = (j % 2) * 130 + hh * 65
                B.op("tensor", "matmul",
                     bank[4 + j // 2][:, co:co + 65],
                     piece[0][hh][kc % 2][:,
                         qt * 128 - qs:qt * 128 - qs + 128],
                     Vtm[kc][:, (2 * hp + hh) * 65:(2 * hp + hh) * 65 + 65],
                     start=False, stop=True, skip_group_check=True,
                     inc=(sPE, 1) if qt == 4 * qg + 3 else None)
            B.mark(f"pv_{hp}_{qg}_{kc}_{hh}", sPE)

    for hp in range(NCK):
        for qg in range(4):
            q0 = qg * 512
            for half in range(2):
                i = 4 + half
                B.bank_acquire("tensor", i)
                B.op("tensor", "matmul", bank[i][:, :], zerosb[:, :],
                     Qfm[0][:, 0:512], start=True, stop=True,
                     skip_group_check=True, inc=(sPE, 1))
                B.mark(f"pz_{hp}_{qg}_{half}", sPE)
            for kc in range(4 * qg + 4):
                qs = max(q0, kc * 128)
                w = q0 + 512 - qs
                diag = kc >= 4 * qg
                par = kc % 2
                for hh in range(2):
                    i = hh * 2 + par
                    B.bank_acquire("tensor", i)
                    B.op("tensor", "matmul", bank[i][:, 0:w],
                         Kfm[hp][hh * 64:(hh + 1) * 64,
                                 kc * 128:(kc + 1) * 128],
                         Qfm[hp][hh * 64:(hh + 1) * 64, qs:qs + w],
                         start=True, stop=not diag,
                         tile_position=(hh * 64, 0),
                         inc=None if diag else (sPE, 1))
                    if diag:
                        B.op("tensor", "matmul", bank[i][:, 0:128],
                             identb[:, :], dmaskb[:, :],
                             start=False, stop=True, inc=(sPE, 1))
                    B.mark(f"sc_{hp}_{qg}_{kc}_{hh}", sPE)
                    B.wait("scalar", f"sc_{hp}_{qg}_{kc}_{hh}")
                    if kc >= 2:
                        B.wait("scalar", f"pv_{hp}_{qg}_{kc - 2}_{hh}")
                    elif qg > 0:
                        B.wait("scalar",
                               f"pv_{hp}_{qg - 1}_{4 * qg - 2 + kc}_{hh}")
                    elif hp > 0:
                        B.wait("scalar", f"pv_{hp - 1}_3_{14 + kc}_{hh}")
                    B.op("scalar", "activation",
                         piece[0][hh][par][:, 0:w], bank[i][:, 0:w], AF.Exp,
                         bias=bias_hd[:, 2 * hp + hh:2 * hp + hh + 1],
                         scale=0.125, inc=(sA, 1))
                    B.mark(f"exp_{hp}_{qg}_{kc}_{hh}", sA)
                    B.bank_release(i, f"exp_{hp}_{qg}_{kc}_{hh}")
                if kc >= 1:
                    pv_emit(hp, qg, kc - 1)
            pv_emit(hp, qg, 4 * qg + 3)
            B.wait("vector", f"pv_{hp}_{qg}_{4 * qg + 3}_1")
            if hp >= 1:
                B.wait("vector", f"atTm_{hp - 1}_3")
            for j in range(4):
                qt = 4 * qg + j
                i2 = 4 + j // 2
                co = (j % 2) * 130
                for hh in range(2):
                    B.op("vector", "reciprocal", rcsc[:, hh:hh + 1],
                         bank[i2][:, co + hh * 65 + 64:co + hh * 65 + 65])
                for hh in range(2):
                    B.op("vector", "tensor_scalar",
                         attn_tm[0][:, qt * 128 + hh * 64:
                                     qt * 128 + hh * 64 + 64],
                         bank[i2][:, co + hh * 65:co + hh * 65 + 64],
                         rcsc[:, hh:hh + 1], None, op0=OP.mult,
                         inc=(sV, 1) if (j % 2 == 1 and hh == 1) else None)
                if j % 2 == 1:
                    B.mark(f"pvev_{hp}_{qg}_{j // 2}", sV)
                    B.bank_release(i2, f"pvev_{hp}_{qg}_{j // 2}")
        B.wait("tensor", f"pvev_{hp}_3_1")
        for qt in range(NTT):
            i = (qt // 4) % 4
            if qt % 4 == 0:
                B.bank_acquire("tensor", i)
            B.op("tensor", "transpose",
                 bb(i)[:, (qt % 4) * 128:(qt % 4) * 128 + 128],
                 attn_tm[0][:, qt * 128:(qt + 1) * 128], identb[:, :],
                 inc=(sPE, 1) if qt % 4 == 3 else None)
            if qt % 4 == 3:
                B.mark(f"atTm_{hp}_{qt // 4}", sPE)
                B.wait("vector", f"atTm_{hp}_{qt // 4}")
                B.op("vector", "tensor_copy",
                     afm[hp][:, (qt - 3) * 128:(qt + 1) * 128],
                     bb(i)[:, 0:512], inc=(sV, 1))
                B.mark(f"afmev_{hp}_{qt // 4}", sV)
                B.bank_release(i, f"afmev_{hp}_{qt // 4}")
    B.mark("attn_done", sV)

    if upto == "attn":
        _finish_debug()
        return _playback_and_close()
    B.wait("tensor", "attn_done")
    B.wait("tensor", "wo")
    B.wait("sync", "kn")
    for tt in range(NTT):
        if tt >= 2:
            B.wait("sync", f"h1_{tt - 2}")
        s_dma(xstage[tt % 2][:, :], dx[tt * 128:(tt + 1) * 128, :],
              ev=f"xf32_{tt}")
    def emit_x1T(tt):
        B.wait("tensor", f"x1b_{tt}")
        for c in range(NCK):
            i = 4 + c // 4
            if c % 4 == 0:
                B.bank_acquire("tensor", i)
            B.op("tensor", "transpose",
                 bb(i)[:, (c % 4) * 128:(c % 4) * 128 + 128],
                 x1bfst[tt % 2][:, c * 128:(c + 1) * 128], identb[:, :],
                 inc=(sPE, 1) if c == NCK - 1 else None)
        B.mark(f"x1T_{tt}", sPE)
        B.wait("vector", f"x1T_{tt}")
        for half in range(2):
            i = 4 + half
            B.op("vector", "tensor_copy",
                 A_v[:, 0:16384].rearrange("p (c t) -> p c t", c=8)
                 [:, half * 4:half * 4 + 4, tt * 128:(tt + 1) * 128],
                 bb(i)[:, 0:512].rearrange("p (c t) -> p c t", c=4),
                 inc=(sV, 1))
            B.mark(f"x1fev_{tt}_{half}", sV)
            B.bank_release(i, f"x1fev_{tt}_{half}")

    bi = 0
    for tt in range(NTT):
        for dog in range(2):
            i = bi % 4
            bk = bank[i]
            B.bank_acquire("tensor", i)
            for dc in range(NCK):
                B.op("tensor", "matmul", bk[:, :],
                     afm[dc][:, tt * 128:(tt + 1) * 128],
                     wslot[1][:, dc, dog * 512:(dog + 1) * 512],
                     start=(dc == 0), stop=False)
            B.op("tensor", "matmul", bk[:, :], ones_b[:, :],
                 outb_row[0:1, dog * 512:(dog + 1) * 512],
                 start=False, stop=True, inc=(sPE, 1))
            B.mark(f"omm{bi}", sPE)
            B.wait("vector", f"omm{bi}")
            if dog == 0:
                B.wait("vector", f"xf32_{tt}")
            if dbg_nores:
                B.op("vector", "tensor_scalar",
                     x1_tm[tt][:, dog * 512:(dog + 1) * 512], bk[:, :],
                     1.0, 0.0, op0=OP.mult, op1=OP.add,
                     accum_out=lnst[:, dog:dog + 1], inc=(sV, 1))
            else:
                B.op("vector", "scalar_tensor_tensor",
                     x1_tm[tt][:, dog * 512:(dog + 1) * 512], bk[:, :], 1.0,
                     xstage[tt % 2][:, dog * 512:(dog + 1) * 512],
                     op0=OP.mult, op1=OP.add,
                     accum_out=lnst[:, dog:dog + 1],
                     inc=(sV, 1))
            B.mark(f"h1ev{bi}", sV)
            B.bank_release(i, f"h1ev{bi}")
            bi += 1
        B.mark(f"h1_{tt}", sV)
        if tt >= 1:
            emit_x1T(tt - 1)
        if dbg_h1 and tt < 2:
            dh = nc.dram_tensor(f"dbg_h1_{tt}", [128, D], F32,
                                kind="ExternalOutput").ap()
            dxs = nc.dram_tensor(f"dbg_xs_{tt}", [128, D], F32,
                                 kind="ExternalOutput").ap()
            B.wait("scalar", f"h1_{tt}")
            B.op("scalar", "dma_start", out=dh, in_=x1_tm[tt][:, :],
                 inc=(sDs, 16))
            B.op("scalar", "dma_start", out=dxs, in_=xstage[tt % 2][:, :],
                 inc=(sDs, 16))
            B.wait_val("vector", sDs, sDs.n)
        B.wait("scalar", f"h1_{tt}")
        B.op("scalar", "activation", x1bfst[tt % 2][:, :], x1_tm[tt][:, :],
             AF.Square, accum_out=lnst[:, 2:3], inc=(sA, 1))
        B.mark(f"sq1_{tt}", sA)
        B.wait("vector", f"sq1_{tt}")
        B.op("vector", "tensor_reduce", lnst[:, 3:4], lnst[:, 0:2], axis=AX.X,
             op=OP.add)
        B.op("vector", "tensor_scalar", lnst[:, 3:4], lnst[:, 3:4], 1.0 / D,
             None, op0=OP.mult)
        B.op("vector", "tensor_scalar", lnst[:, 4:5], lnst[:, 3:4],
             lnst[:, 3:4], None, op0=OP.mult)
        B.op("vector", "scalar_tensor_tensor", lnst[:, 5:6], lnst[:, 2:3],
             1.0 / D, lnst[:, 4:5], op0=OP.mult, op1=OP.subtract)
        B.op("vector", "tensor_scalar", lnst[:, 5:6], lnst[:, 5:6], LN_EPS,
             None, op0=OP.add)
        B.op("vector", "reciprocal", lnst[:, 6:7], lnst[:, 5:6], inc=(sV, 1))
        B.mark(f"st1a_{tt}", sV)
        B.wait("scalar", f"st1a_{tt}")
        B.op("scalar", "activation", lnst[:, 7:8], lnst[:, 6:7], AF.Sqrt,
             inc=(sA, 1))
        B.mark(f"st1b_{tt}", sA)
        B.wait("vector", f"st1b_{tt}")
        if not dbg_nores:
            B.op("vector", "tensor_scalar", x1_tm[tt][:, :], x1_tm[tt][:, :],
                 lnst[:, 3:4], lnst[:, 7:8], op0=OP.subtract, op1=OP.mult)
        B.op("vector", "tensor_copy", x1bfst[tt % 2][:, :], x1_tm[tt][:, :],
             inc=(sV, 1))
        B.mark(f"x1b_{tt}", sV)
    emit_x1T(NTT - 1)
    B.mark("omm_done", sPE)

    if upto == "ln1":
        _finish_debug()
        return _playback_and_close()
    B.wait("tensor", "smallw")
    for tt in range(NTT):
        B.wait("tensor", f"x1fev_{tt}_1")
        i = 6
        B.bank_acquire("tensor", i)
        for dc in range(NCK):
            B.op("tensor", "matmul", bank[i][:, 0:NE],
                 x1fm[dc][:, tt * 128:(tt + 1) * 128],
                 gw_sb[:, dc * NE:(dc + 1) * NE],
                 start=(dc == 0), stop=False)
        B.op("tensor", "matmul", bank[i][:, 0:NE], ones_b[:, :],
             gb_row[0:1, :], start=False, stop=True, inc=(sPE, 1))
        B.mark(f"gmm_{tt}", sPE)
        B.wait("vector", f"gmm_{tt}")
        sc = scores_g[:, tt * NE:(tt + 1) * NE]
        B.op("vector", "tensor_scalar", sc, bank[i][:, 0:NE], EPS_GATE, None,
             op0=OP.max, inc=(sV, 1))
        B.mark(f"gev_{tt}", sV)
        B.bank_release(i, f"gev_{tt}")
        B.op("vector", "reduce_max", gsc[:, 0:1], sc, axis=AX.X, negate=True,
             inc=(sV, 1))
        B.mark(f"gmx_{tt}", sV)
        B.wait("scalar", f"gmx_{tt}")
        B.op("scalar", "activation", p_g[:, :], sc, AF.Exp,
             bias=gsc[:, 0:1], scale=1.0, accum_out=gsc[:, 1:2], inc=(sA, 1))
        B.mark(f"gexp_{tt}", sA)
        B.wait("vector", f"gexp_{tt}")
        B.op("vector", "reciprocal", gsc[:, 2:3], gsc[:, 1:2])
        B.op("vector", "tensor_scalar", p_g[:, :], p_g[:, :], gsc[:, 2:3],
             None, op0=OP.mult)
        B.op("vector", "reduce_max", gsc[:, 3:4], p_g[:, :], axis=AX.X,
             negate=True)
        B.op("vector", "tensor_scalar", g8a[:, :], p_g[:, :], gsc[:, 3:4],
             None, op0=OP.add)
        B.op("vector", "tensor_single_scalar", g8a[:, :], g8a[:, :], 0.0,
             OP.is_ge)
        B.op("vector", "tensor_scalar", g8a[:, :], g8a[:, :], -BIG, BIG,
             op0=OP.mult, op1=OP.add)
        B.op("vector", "tensor_tensor", g8a[:, :], g8a[:, :], idx8[:, :],
             op=OP.add)
        B.op("vector", "tensor_reduce", gsc[:, 4:5], g8a[:, :], axis=AX.X,
             op=OP.min, negate=True)
        B.op("vector", "tensor_scalar", g8a[:, :], idx8[:, :], gsc[:, 4:5],
             None, op0=OP.add)
        B.op("vector", "tensor_single_scalar", g8a[:, :], g8a[:, :], 0.0,
             OP.is_equal)
        B.op("vector", "scalar_tensor_tensor", g8b[:, :], g8a[:, :], -BIG,
             p_g[:, :], op0=OP.mult, op1=OP.add)
        B.op("vector", "reduce_max", gsc[:, 5:6], g8b[:, :], axis=AX.X,
             negate=True)
        B.op("vector", "tensor_scalar", g8b[:, :], g8b[:, :], gsc[:, 5:6],
             None, op0=OP.add)
        B.op("vector", "tensor_single_scalar", g8b[:, :], g8b[:, :], 0.0,
             OP.is_ge)
        B.op("vector", "tensor_scalar", g8b[:, :], g8b[:, :], -BIG, BIG,
             op0=OP.mult, op1=OP.add)
        B.op("vector", "tensor_tensor", g8b[:, :], g8b[:, :], idx8[:, :],
             op=OP.add)
        B.op("vector", "tensor_reduce", gsc[:, 6:7], g8b[:, :], axis=AX.X,
             op=OP.min, negate=True)
        B.op("vector", "tensor_scalar", g8b[:, :], idx8[:, :], gsc[:, 6:7],
             None, op0=OP.add)
        B.op("vector", "tensor_single_scalar", g8b[:, :], g8b[:, :], 0.0,
             OP.is_equal)
        B.op("vector", "tensor_tensor", g8a[:, :], g8a[:, :], g8b[:, :],
             op=OP.add)
        cb = comb_g[:, tt * NE:(tt + 1) * NE]
        B.op("vector", "tensor_scalar", cb, p_g[:, :], W_THRES, None,
             op0=OP.max)
        B.op("vector", "tensor_tensor", cb, cb, g8a[:, :], op=OP.mult)
        B.op("vector", "tensor_reduce", gsc[:, 7:8], cb, axis=AX.X, op=OP.add)
        B.op("vector", "reciprocal", gsc[:, 8:9], gsc[:, 7:8])
        B.op("vector", "tensor_scalar", cb, cb, gsc[:, 8:9], 0.5,
             op0=OP.mult, op1=OP.mult)
        B.op("vector", "tensor_copy", combbf[:, tt * NE:(tt + 1) * NE], cb,
             inc=(sV, 1))
        B.mark(f"comb_{tt}", sV)

    if upto == "gate":
        _finish_debug()
        return _playback_and_close()
    def emit_combT():
        B.wait("tensor", "smallw")
        for tt in range(NTT):
            B.wait("tensor", f"comb_{tt}")
            i = 6 + tt % 2
            B.bank_acquire("tensor", i)
            B.op("tensor", "transpose", bb(i)[0:NE, 0:128],
                 combbf[:, tt * NE:(tt + 1) * NE], identb[:, :],
                 inc=(sPE, 1))
            B.mark(f"combT_{tt}", sPE)
            B.wait("vector", f"combT_{tt}")
            B.op("vector", "tensor_copy", combT[:, tt * 128:(tt + 1) * 128],
                 bb(i)[0:NE, 0:128], inc=(sV, 1))
            B.mark(f"combTev_{tt}", sV)
            B.bank_release(i, f"combTev_{tt}")

    bi = 0
    for e in range(NE):
        B.wait("tensor", f"we{e}")
        ws = wslot[e % 2]
        for tt in range(NTT):
            for dog in range(2):
                i = bi % 6
                B.bank_acquire("tensor", i)
                for dc in range(NCK):
                    B.op("tensor", "matmul", bank[i][:, :],
                         x1fm[dc][:, tt * 128:(tt + 1) * 128],
                         ws[:, dc, dog * 512:(dog + 1) * 512],
                         start=(dc == 0), stop=(dc == NCK - 1),
                         inc=(sPE, 1) if dc == NCK - 1 else None)
                B.mark(f"emm{bi}", sPE)
                B.wait("vector", f"emm{bi}")
                if e == 0:
                    B.op("vector", "tensor_scalar",
                         acc[tt][:, dog * 512:(dog + 1) * 512], bank[i][:, :],
                         comb_g[:, tt * NE + e:tt * NE + e + 1], None,
                         op0=OP.mult, inc=(sV, 1))
                else:
                    B.op("vector", "scalar_tensor_tensor",
                         acc[tt][:, dog * 512:(dog + 1) * 512], bank[i][:, :],
                         comb_g[:, tt * NE + e:tt * NE + e + 1],
                         acc[tt][:, dog * 512:(dog + 1) * 512],
                         op0=OP.mult, op1=OP.add, inc=(sV, 1))
                B.mark(f"eev{bi}", sV)
                B.bank_release(i, f"eev{bi}")
                bi += 1
        B.mark(f"expmm{e}", sPE)
        if e == 0:
            emit_combT()
    for tt in range(NTT):
        for dog in range(2):
            i = bi % 6
            B.bank_acquire("tensor", i)
            B.op("tensor", "matmul", bank[i][:, :],
                 combT[:, tt * 128:(tt + 1) * 128],
                 eb_sb[:, dog * 512:(dog + 1) * 512],
                 start=True, stop=True, inc=(sPE, 1))
            B.mark(f"emm{bi}", sPE)
            B.wait("vector", f"emm{bi}")
            B.op("vector", "scalar_tensor_tensor",
                 acc[tt][:, dog * 512:(dog + 1) * 512], bank[i][:, :], 1.0,
                 acc[tt][:, dog * 512:(dog + 1) * 512],
                 op0=OP.mult, op1=OP.add, inc=(sV, 1))
            B.mark(f"eev{bi}", sV)
            B.bank_release(i, f"eev{bi}")
            bi += 1

    if upto == "ffn":
        _finish_debug()
        return _playback_and_close()
    for tt in range(NTT):
        B.op("vector", "scalar_tensor_tensor", acc[tt][:, 0:512],
             acc[tt][:, 0:512], 1.0, x1_tm[tt][:, 0:512],
             op0=OP.mult, op1=OP.add, accum_out=lnst[:, 0:1])
        B.op("vector", "scalar_tensor_tensor", acc[tt][:, 512:1024],
             acc[tt][:, 512:1024], 1.0, x1_tm[tt][:, 512:1024],
             op0=OP.mult, op1=OP.add, accum_out=lnst[:, 1:2], inc=(sV, 1))
        B.mark(f"h2_{tt}", sV)
        B.wait("scalar", f"h2_{tt}")
        if tt == 0:
            B.bank_acquire("scalar", 6)
            B.bank_acquire("scalar", 7)
        B.op("scalar", "activation", bank[6][:, :], acc[tt][:, 0:512],
             AF.Square, accum_out=lnst[:, 2:3])
        B.op("scalar", "activation", bank[7][:, :], acc[tt][:, 512:1024],
             AF.Square, accum_out=lnst[:, 4:5], inc=(sA, 1))
        B.mark(f"sq2_{tt}", sA)
        B.wait("vector", f"sq2_{tt}")
        B.op("vector", "tensor_reduce", lnst[:, 3:4], lnst[:, 0:2], axis=AX.X,
             op=OP.add)
        B.op("vector", "tensor_scalar", lnst[:, 3:4], lnst[:, 3:4], 1.0 / D,
             None, op0=OP.mult)
        B.op("vector", "tensor_tensor", lnst[:, 2:3], lnst[:, 2:3],
             lnst[:, 4:5], op=OP.add)
        B.op("vector", "tensor_scalar", lnst[:, 4:5], lnst[:, 3:4],
             lnst[:, 3:4], None, op0=OP.mult)
        B.op("vector", "scalar_tensor_tensor", lnst[:, 5:6], lnst[:, 2:3],
             1.0 / D, lnst[:, 4:5], op0=OP.mult, op1=OP.subtract)
        B.op("vector", "tensor_scalar", lnst[:, 5:6], lnst[:, 5:6], LN_EPS,
             None, op0=OP.add)
        B.op("vector", "reciprocal", lnst[:, 6:7], lnst[:, 5:6], inc=(sV, 1))
        B.mark(f"st2a_{tt}", sV)
        B.wait("scalar", f"st2a_{tt}")
        B.op("scalar", "activation", lnst[:, 7:8], lnst[:, 6:7], AF.Sqrt,
             inc=(sA, 1))
        B.mark(f"st2b_{tt}", sA)
        B.wait("vector", f"st2b_{tt}")
        B.op("vector", "tensor_scalar", acc[tt][:, :], acc[tt][:, :],
             lnst[:, 3:4], lnst[:, 7:8], op0=OP.subtract, op1=OP.mult,
             inc=(sV, 1))
        B.mark(f"out_{tt}", sV)
        B.wait("sync", f"out_{tt}")
        s_dma(dy[tt * 128:(tt + 1) * 128, :], acc[tt][:, :], ev=f"st_{tt}")
    B.wait_val("sync", sDs, sDs.n)

    return _playback_and_close()


_NC_CACHE = {}


def _get_nc():
    if "nc" not in _NC_CACHE:
        _NC_CACHE["nc"] = build_program()
    return _NC_CACHE["nc"]


def _consts():
    identf = np.eye(128, dtype=np.float32)
    identb = identf.astype(bf16)
    k = np.arange(128)[:, None]
    q = np.arange(128)[None, :]
    dmask = np.where(k > q, NEG, 0.0).astype(np.float32).astype(bf16)
    sel = np.zeros((128, 128), np.float32)
    for c in range(8):
        sel[0:64, c * 16 + 2 * c] = 1.0
        sel[64:128, c * 16 + 2 * c + 1] = 1.0
    selb = sel.astype(bf16)
    idx8 = np.broadcast_to(np.arange(8, dtype=np.float32), (128, 8)).copy()
    return identf, identb, dmask, selb, idx8


def make_in_maps(x, wq, wk, wv, wo, outb, gw, gb, ew, eb):
    identf, identb, dmask, selb, idx8 = _consts()
    in_maps = []
    for i in range(8):
        in_maps.append({
            "x": np.ascontiguousarray(x[i]),
            "wq": wq, "wk": wk, "wv": wv, "wo": wo, "outb": outb,
            "gw": gw, "gb": gb, "ew": ew, "eb": eb,
            "identf": identf, "identb": identb, "dmask": dmask,
            "sel": selb, "idx8": idx8,
        })
    return in_maps


def kernel(L0_attn_batch, L0_attn_w_query, L0_attn_w_key, L0_attn_w_value,
           L0_attn_out_w, L0_attn_out_b, L0_ffn_gate_W, L0_ffn_gate_b,
           L0_ffn_expert_W, L0_ffn_expert_b):
    x = np.ascontiguousarray(np.asarray(L0_attn_batch, dtype=np.float32))
    wq = np.ascontiguousarray(np.asarray(L0_attn_w_query, dtype=np.float32))
    wk = np.ascontiguousarray(np.asarray(L0_attn_w_key, dtype=np.float32))
    wv = np.ascontiguousarray(np.asarray(L0_attn_w_value, dtype=np.float32))
    wo = np.ascontiguousarray(np.asarray(L0_attn_out_w, dtype=np.float32))
    outb = np.ascontiguousarray(
        np.asarray(L0_attn_out_b, dtype=np.float32)).reshape(1, D)
    gw = np.ascontiguousarray(np.asarray(L0_ffn_gate_W, dtype=np.float32))
    gb = np.ascontiguousarray(
        np.asarray(L0_ffn_gate_b, dtype=np.float32)).reshape(1, NE)
    ew = np.ascontiguousarray(np.asarray(L0_ffn_expert_W, dtype=np.float32))
    eb = np.ascontiguousarray(np.asarray(L0_ffn_expert_b, dtype=np.float32))
    nc = _get_nc()
    in_maps = make_in_maps(x, wq, wk, wv, wo, outb, gw, gb, ew, eb)
    res = run_bass_kernel_spmd(nc, in_maps, core_ids=list(range(8)))
    out = np.stack([np.asarray(r["y"], dtype=np.float32)
                    for r in res.results], axis=0)
    return out


# version 04 (final)
# speedup vs baseline: 1.6638x; 1.1183x over previous
import numpy as np
import ml_dtypes
from contextlib import ExitStack
import concourse.bass as bass
import concourse.mybir as mybir
from concourse.bass_utils import run_bass_kernel_spmd

F32 = mybir.dt.float32
BF16 = mybir.dt.bfloat16
OP = mybir.AluOpType
AF = mybir.ActivationFunctionType
AX = mybir.AxisListType
bf16 = ml_dtypes.bfloat16

S = 2048
D = 1024
H = 16
HD = 64
NCK = 8
NTT = 16
NE = 8
EPS_GATE = 1e-4
W_THRES = 0.1
LN_EPS = 1e-5
NEG = -1e9
BIG = 1e9

ENGS = ("tensor", "vector", "scalar", "sync", "gpsimd")


class _Sem:
    def __init__(self, name):
        self.name = name
        self.n = 0
        self.h = None


def _ap_iv(x):
    if not (hasattr(x, "tensor") and hasattr(x, "offset") and hasattr(x, "ap")):
        return None
    try:
        esz = mybir.dt.size(x.dtype)
        dims = list(x.ap)
        ext = 1
        for step, count in dims[1:]:
            ext += abs(int(step)) * (int(count) - 1)
        lo = int(x.offset) * esz
        return (x.tensor.name, lo, lo + ext * esz)
    except Exception:
        return None


_HAZ_ENGS = ("vector", "scalar", "gpsimd")
_NO_TRACK = {"dma_start", "nop", "drain", "memzero"}
_WINDOW = 12


class Builder:
    def __init__(self, nc, lenient=False):
        self.nc = nc
        self.lenient = lenient
        self.code = {e: [] for e in ENGS}
        self.events = {}
        self.sems = {}
        self.bank_free = {}
        self.recent = {e: [] for e in _HAZ_ENGS}
        self.n_drains = 0

    def sem(self, name):
        s = _Sem(name)
        self.sems[name] = s
        return s

    def op(self, eng, meth, *a, inc=None, **kw):
        if eng in _HAZ_ENGS and meth not in _NO_TRACK:
            writes, reads = [], []
            for i, x in enumerate(a):
                iv = _ap_iv(x)
                if iv is None:
                    continue
                (writes if i == 0 else reads).append(iv)
            for k2, x in kw.items():
                iv = _ap_iv(x)
                if iv is None:
                    continue
                (writes if k2 in ("out", "accum_out") else reads).append(iv)
            if meth == "memset":
                pass
            hit = False
            for wr, nm, lo, hi in ([(1, *w) for w in writes] +
                                   [(0, *r) for r in reads]):
                for owr, onm, olo, ohi in self.recent[eng]:
                    if (wr or owr) and nm == onm and lo < ohi and olo < hi:
                        hit = True
                        break
                if hit:
                    break
            if hit:
                self.code[eng].append(("op", "drain", (), {}, None))
                self.recent[eng] = []
                self.n_drains += 1
            self.recent[eng].extend([(1, *w) for w in writes] +
                                    [(0, *r) for r in reads])
            self.recent[eng] = self.recent[eng][-_WINDOW:]
        if inc is not None:
            sem, k = inc
            sem.n += k
        self.code[eng].append(("op", meth, a, kw, inc))

    def wait(self, eng, ev):
        assert isinstance(ev, str)
        self.code[eng].append(("wait", ev))

    def wait_val(self, eng, sem, val):
        self.code[eng].append(("waitv", sem, val))

    def mark(self, ev, sem):
        assert ev not in self.events, ev
        self.events[ev] = (sem, sem.n)

    def bank_acquire(self, eng, idx):
        ev = self.bank_free.get(idx)
        if ev is not None:
            self.wait(eng, ev)

    def bank_release(self, idx, ev):
        self.bank_free[idx] = ev

    def playback(self, eng_name, eng):
        for item in self.code[eng_name]:
            if item[0] == "op":
                _, meth, a, kw, inc = item
                inst = getattr(eng, meth)(*a, **kw)
                if inc is not None:
                    sem, k = inc
                    inst.then_inc(sem.h, k)
            elif item[0] == "wait":
                if item[1] not in self.events:
                    assert self.lenient, item[1]
                    continue
                sem, val = self.events[item[1]]
                if val > 0:
                    eng.wait_ge(sem.h, val)
            else:
                _, sem, val = item
                if val > 0:
                    eng.wait_ge(sem.h, val)


def ceil_div(a, b):
    return (a + b - 1) // b


def build_program(upto=None, dumps=(), dbg_h1=False, dbg_nores=False):
    nc = bass.Bass("TRN2", target_bir_lowering=False)
    B = Builder(nc, lenient=upto is not None)

    dx = nc.dram_tensor("x", [S, D], F32, kind="ExternalInput").ap()
    dwq = nc.dram_tensor("wq", [D, D], F32, kind="ExternalInput").ap()
    dwk = nc.dram_tensor("wk", [D, D], F32, kind="ExternalInput").ap()
    dwv = nc.dram_tensor("wv", [D, D], F32, kind="ExternalInput").ap()
    dwo = nc.dram_tensor("wo", [D, D], F32, kind="ExternalInput").ap()
    doutb = nc.dram_tensor("outb", [1, D], F32, kind="ExternalInput").ap()
    dgw = nc.dram_tensor("gw", [D, NE], F32, kind="ExternalInput").ap()
    dgb = nc.dram_tensor("gb", [1, NE], F32, kind="ExternalInput").ap()
    dew = nc.dram_tensor("ew", [NE, D, D], F32, kind="ExternalInput").ap()
    deb = nc.dram_tensor("eb", [NE, D], F32, kind="ExternalInput").ap()
    didf = nc.dram_tensor("identf", [128, 128], F32, kind="ExternalInput").ap()
    didb = nc.dram_tensor("identb", [128, 128], BF16, kind="ExternalInput").ap()
    dmask = nc.dram_tensor("dmask", [128, 128], BF16, kind="ExternalInput").ap()
    dsel = nc.dram_tensor("sel", [128, 128], BF16, kind="ExternalInput").ap()
    didx = nc.dram_tensor("idx8", [128, NE], F32, kind="ExternalInput").ap()
    dy = nc.dram_tensor("y", [S, D], F32, kind="ExternalOutput").ap()

    ctx = ExitStack()
    SB = lambda name, shape, dt: ctx.enter_context(nc.sbuf_tensor(name, shape, dt))

    A_w = SB("A_w", [128, 16384], BF16)
    A_qk = SB("A_qk", [128, 32768], BF16)
    A_v = SB("A_v", [128, 16640], BF16)
    A_xa = SB("A_xa", [128, 32768], BF16)

    def carve(arena, byte_off, nparts, dt, nfree):
        if dt == BF16:
            assert byte_off % 2 == 0
            return arena[0:nparts, byte_off // 2: byte_off // 2 + nfree]
        assert byte_off % 4 == 0
        return arena[:, :].bitcast(F32)[0:nparts, byte_off // 4: byte_off // 4 + nfree]

    wslot = [carve(A_w, i * 16384, 128, BF16, 8192).rearrange(
        "p (c o) -> p c o", c=NCK) for i in range(2)]
    Qfm = [carve(A_qk, c * 4096, 128, BF16, 2048) for c in range(NCK)]
    Kfm = [carve(A_qk, 32768 + c * 4096, 128, BF16, 2048) for c in range(NCK)]
    x1_tm = [carve(A_qk, tt * 4096, 128, F32, 1024) for tt in range(NTT)]
    Vtm = [carve(A_v, tt * 2080, 128, BF16, 1040) for tt in range(NTT)]
    xbstage = [carve(A_v, i * 2048, 128, BF16, 1024) for i in range(2)]
    x1fm = [carve(A_v, c * 4096, 128, BF16, 2048) for c in range(NCK)]
    xfm = [carve(A_xa, c * 4096, 128, BF16, 2048) for c in range(NCK)]
    afm = xfm
    WK = 32768
    piece = [[[carve(A_xa, WK + (((ps * 2 + hh) * 2 + pr) * 1024), 128, BF16,
                     512) for pr in range(2)] for hh in range(2)]
             for ps in range(2)]
    attn_tm = [carve(A_xa, WK + 8192 + ps * 4096, 128, BF16, 2048)
               for ps in range(2)]
    qsq2 = [carve(A_xa, WK + 16384, 128, BF16, 2048),
            carve(A_xa, WK + 28672, 128, BF16, 2048)]
    qkn2 = carve(A_xa, WK + 20480, 16, F32, 2048)
    xstage = [carve(A_xa, WK + 16384 + i * 4096, 128, F32, 1024)
              for i in range(2)]
    x1bfst = [carve(A_xa, WK + 24576 + i * 2048, 128, BF16, 1024)
              for i in range(2)]
    acc = [carve(A_xa, tt * 4096, 128, F32, 1024) for tt in range(NTT)]

    identf = SB("identf_sb", [128, 128], F32)
    identb = SB("identb_sb", [128, 128], BF16)
    dmaskb = SB("dmask_sb", [128, 128], BF16)
    selb = SB("sel_sb", [128, 128], BF16)
    idx8 = SB("idx8_sb", [128, NE], F32)
    ones_f = SB("ones_f", [1, 128], F32)
    ones_b = SB("ones_b", [1, 128], BF16)
    outb_row = SB("outb_row", [1, D], BF16)
    gw_sb = SB("gw_sb", [128, NCK * NE], BF16)
    gb_row = SB("gb_row", [1, NE], BF16)
    eb_sb = SB("eb_sb", [NE, D], BF16)
    bias_hd = SB("bias_hd", [128, 16], F32)
    maxk_row = SB("maxk_row", [1, 16], F32)
    zerosb = SB("zerosb", [128, 128], BF16)
    rcsc = SB("rcsc", [128, 8], F32)
    combT = SB("combT", [NE, S], BF16)
    scores_g = SB("scores_g", [128, NTT * NE], F32)
    p_g = SB("p_g", [128, NE], F32)
    comb_g = SB("comb_g", [128, NTT * NE], F32)
    gsc = SB("gsc", [128, 16], F32)
    g8a = SB("g8a", [128, NE], F32)
    g8b = SB("g8b", [128, NE], F32)
    lnst = SB("lnst", [128, 8], F32)
    combbf = SB("combbf", [128, NTT * NE], BF16)
    dbgst = SB("dbgst", [128, 128], F32)

    bank = [ctx.enter_context(nc.psum_tensor(f"bank{i}", [128, 512], F32))
            for i in range(8)]

    def bb(i):
        return bank[i][:, :].bitcast(BF16)

    sDg = B.sem("sDg")
    sDs = B.sem("sDs")
    sPE = B.sem("sPE")
    sV = B.sem("sV")
    sA = B.sem("sA")

    def g_dma(out, in_, ev=None):
        B.op("gpsimd", "dma_start", out=out, in_=in_, inc=(sDg, 16))
        if ev:
            B.mark(ev, sDg)

    def s_dma(out, in_, ev=None):
        B.op("sync", "dma_start", out=out, in_=in_, inc=(sDs, 16))
        if ev:
            B.mark(ev, sDs)


    def _finish_debug():
        B.wait_val("sync", sPE, sPE.n)
        B.wait_val("sync", sV, sV.n)
        B.wait_val("sync", sA, sA.n)
        B.wait_val("sync", sDg, sDg.n)
        dump_views = {
            "xfm": xfm, "Qfm": Qfm, "Kfm": Kfm, "Vtm": Vtm, "afm": afm,
            "x1_tm": x1_tm, "x1fm": x1fm, "acc": acc,
            "bias_all": [bias_hd[:, :]],
            "comb_g": [comb_g[:, :]], "combT": [combT[:, :]],
            "scores_g": [scores_g[:, :]], "dbgst": [dbgst[:, :]],
        }
        for dn in dumps:
            views = dump_views[dn]
            for vi, ap in enumerate(views):
                shp = list(ap.shape)
                dt = ap.dtype
                dout = nc.dram_tensor(f"dump_{dn}_{vi}", shp, dt,
                                      kind="ExternalOutput").ap()
                s_dma(dout, ap)
        B.wait_val("sync", sDs, sDs.n)

    def _playback_and_close():
        for s in B.sems.values():
            s.h = ctx.enter_context(nc.semaphore(s.name))
        with nc.Block() as block:
            @block.tensor
            def _(t):
                B.playback("tensor", t)

            @block.vector
            def _(v):
                B.playback("vector", v)

            @block.scalar
            def _(s):
                B.playback("scalar", s)

            @block.sync
            def _(s):
                B.playback("sync", s)

            @block.gpsimd
            def _(g):
                B.playback("gpsimd", g)
        ctx.close()
        return nc

    s_dma(identf[:, :], didf)
    s_dma(identb[:, :], didb)
    s_dma(dmaskb[:, :], dmask)
    s_dma(selb[:, :], dsel)
    s_dma(idx8[:, :], didx, ev="consts")
    B.op("gpsimd", "memset", ones_f[:, :], 1.0)
    B.op("gpsimd", "memset", zerosb[:, :], 0.0)
    B.op("gpsimd", "memset", ones_b[:, :], 1.0, inc=(sDg, 1))
    B.mark("ones", sDg)

    g_dma(wslot[0][:, :, :], dwq.rearrange("(c p) o -> p c o", p=128),
          ev="wq")
    for tt in range(NTT):
        if tt >= 2:
            B.wait("gpsimd", f"xtr{tt - 2}")
        g_dma(xbstage[tt % 2][:, :], dx[tt * 128:(tt + 1) * 128, :],
              ev=f"xld{tt}")
    g_dma(wslot[1][:, :, :], dwk.rearrange("(c p) o -> p c o", p=128), ev="wk")
    g_dma(gw_sb[:, :].rearrange("p (c e) -> p c e", c=NCK),
          dgw.rearrange("(c p) e -> p c e", p=128))
    g_dma(gb_row[:, :], dgb)
    g_dma(outb_row[:, :], doutb)
    g_dma(eb_sb[:, :], deb, ev="smallw")
    B.wait("gpsimd", "qmm_done")
    g_dma(wslot[0][:, :, :], dwv.rearrange("(c p) o -> p c o", p=128), ev="wv")
    B.wait("gpsimd", "kmm_done")
    g_dma(wslot[1][:, :, :], dwo.rearrange("(c p) o -> p c o", p=128), ev="wo")
    B.wait("gpsimd", f"xtr{NTT - 1}")
    B.op("gpsimd", "memset",
         A_v[:, :].rearrange("p (t x) -> p t x", t=NTT)[:, :, 64:1040:65],
         1.0, inc=(sDg, 1))
    B.mark("vones", sDg)
    for e in range(NE):
        if e == 0:
            B.wait("gpsimd", "vmm_done")
        elif e == 1:
            B.wait("gpsimd", "omm_done")
        else:
            B.wait("gpsimd", f"expmm{e - 2}")
        g_dma(wslot[e % 2][:, :, :],
              dew[e].rearrange("(c p) o -> p c o", p=128), ev=f"we{e}")

    B.wait("tensor", "ones")
    B.wait("tensor", "consts")
    for tt in range(NTT):
        B.wait("tensor", f"xld{tt}")
        for c in range(NCK):
            i = 6 + c // 4
            if c % 4 == 0:
                B.bank_acquire("tensor", i)
            B.op("tensor", "transpose", bb(i)[:, (c % 4) * 128:(c % 4) * 128 + 128],
                 xbstage[tt % 2][:, c * 128:(c + 1) * 128], identb[:, :],
                 inc=(sPE, 1) if c == NCK - 1 else None)
        B.mark(f"xtr{tt}", sPE)
        B.wait("vector", f"xtr{tt}")
        for half in range(2):
            i = 6 + half
            B.op("vector", "tensor_copy",
                 A_xa[:, :].rearrange("p (c t) -> p c t", c=16)
                 [:, half * 4:half * 4 + 4, tt * 128:(tt + 1) * 128],
                 bb(i)[:, 0:512].rearrange("p (c t) -> p c t", c=4),
                 inc=(sV, 1))
            B.mark(f"xev{tt}_{half}", sV)
            B.bank_release(i, f"xev{tt}_{half}")

    if upto == "xfm":
        _finish_debug()
        return _playback_and_close()
    def proj_fm(name, slot_i, dst, w_ev):
        B.wait("tensor", w_ev)
        bi = 0
        for oc in range(NCK):
            for tg in range(4):
                bk = bank[bi % 4]
                B.wait("tensor", f"xev{4 * tg + 3}_1")
                B.bank_acquire("tensor", bi % 4)
                for dc in range(NCK):
                    B.op("tensor", "matmul", bk[:, :],
                         wslot[slot_i][:, dc, oc * 128:(oc + 1) * 128],
                         xfm[dc][:, tg * 512:(tg + 1) * 512],
                         start=(dc == 0), stop=(dc == NCK - 1),
                         inc=(sPE, 1) if dc == NCK - 1 else None)
                B.mark(f"{name}mm{bi}", sPE)
                B.wait("vector", f"{name}mm{bi}")
                B.op("vector", "tensor_copy",
                     dst[oc][:, tg * 512:(tg + 1) * 512], bk[:, :], inc=(sV, 1))
                B.mark(f"{name}ev{bi}", sV)
                B.bank_release(bi % 4, f"{name}ev{bi}")
                bi += 1
        B.mark(f"{name}mm_done", sPE)
        B.mark(f"{name}ev_done", sV)

    proj_fm("q", 0, Qfm, "wq")
    proj_fm("k", 1, Kfm, "wk")

    B.wait("tensor", "wv")
    bi = 0
    for tt in range(NTT):
        for dog in range(2):
            bk = bank[bi % 4]
            B.bank_acquire("tensor", bi % 4)
            for dc in range(NCK):
                B.op("tensor", "matmul", bk[:, :],
                     xfm[dc][:, tt * 128:(tt + 1) * 128],
                     wslot[0][:, dc, dog * 512:(dog + 1) * 512],
                     start=(dc == 0), stop=(dc == NCK - 1),
                     inc=(sPE, 1) if dc == NCK - 1 else None)
            B.mark(f"vmm{bi}", sPE)
            B.wait("vector", f"vmm{bi}")
            B.op("vector", "tensor_copy",
                 Vtm[tt][:, dog * 8 * 65:dog * 8 * 65 + 520]
                 .rearrange("p (h d) -> p h d", h=8)[:, :, 0:64],
                 bk[:, :].rearrange("p (h d) -> p h d", h=8), inc=(sV, 1))
            B.mark(f"vev{bi}", sV)
            B.bank_release(bi % 4, f"vev{bi}")
            bi += 1
    B.mark("vmm_done", sPE)

    if upto == "qkv":
        _finish_debug()
        return _playback_and_close()
    def side_norms(name, src, src_ev_v, prev_pe_ev):
        B.wait("scalar", src_ev_v)
        for c in range(NCK):
            if c >= 2:
                B.wait("scalar", f"{name}n2c{c - 2}")
            elif prev_pe_ev is not None:
                B.wait("scalar", f"{prev_pe_ev}{6 + c}")
            B.op("scalar", "activation", qsq2[c % 2][:, :], src[c][:, :],
                 AF.Square, inc=(sA, 1))
            B.mark(f"{name}sq{c}", sA)
            B.wait("tensor", f"{name}sq{c}")
            for tg in range(4):
                if c == 0:
                    B.bank_acquire("tensor", 4 + tg)
                B.op("tensor", "matmul", bank[4 + tg][0:16, :],
                     selb[:, c * 16:(c + 1) * 16],
                     qsq2[c % 2][:, tg * 512:(tg + 1) * 512],
                     start=(c == 0), stop=(c == NCK - 1),
                     inc=(sPE, 1) if tg == 3 else None)
            B.mark(f"{name}n2c{c}", sPE)
        B.wait("vector", f"{name}n2c{NCK - 1}")
        for tg in range(4):
            B.op("vector", "tensor_copy", qkn2[:, tg * 512:(tg + 1) * 512],
                 bank[4 + tg][0:16, :], inc=(sV, 1))
            B.mark(f"{name}n2e{tg}", sV)
            B.bank_release(4 + tg, f"{name}n2e{tg}")
        B.wait("scalar", f"{name}n2e3")
        B.op("scalar", "activation", qkn2[:, :], qkn2[:, :], AF.Sqrt,
             inc=(sA, 1))
        B.mark(f"{name}n", sA)

    side_norms("q", Qfm, "qev_done", None)
    B.wait("vector", "qn")
    B.op("vector", "reduce_max", gsc[0:16, 0:1], qkn2[:, :], axis=AX.X,
         inc=(sV, 1))
    B.mark("maxq", sV)
    side_norms("k", Kfm, "kev_done", "qn2c")
    B.wait("vector", "kn")
    B.op("vector", "reduce_max", gsc[0:16, 1:2], qkn2[:, :], axis=AX.X)
    B.op("vector", "tensor_tensor", gsc[0:16, 2:3], gsc[0:16, 0:1],
         gsc[0:16, 1:2], op=OP.mult)
    B.op("vector", "tensor_scalar", gsc[0:16, 3:4], gsc[0:16, 2:3], -0.125,
         None, op0=OP.mult, inc=(sV, 1))
    B.mark("maxqk", sV)
    B.wait("tensor", "maxqk")
    B.bank_acquire("tensor", 4)
    B.op("tensor", "transpose", bank[4][0:1, 0:16], gsc[0:16, 3:4],
         identf[0:16, 0:16], inc=(sPE, 1))
    B.mark("maxkT", sPE)
    B.wait("vector", "maxkT")
    B.op("vector", "tensor_copy", maxk_row[:, :], bank[4][0:1, 0:16],
         inc=(sV, 1))
    B.mark("maxkrow", sV)
    B.bank_release(4, "maxkrow")
    B.wait("tensor", "maxkrow")
    B.bank_acquire("tensor", 5)
    B.op("tensor", "matmul", bank[5][:, 0:16], ones_f[:, :], maxk_row[:, :],
         start=True, stop=True, inc=(sPE, 1))
    B.mark("maxkbcmm", sPE)
    B.wait("vector", "maxkbcmm")
    B.op("vector", "tensor_copy", bias_hd[:, :], bank[5][:, 0:16],
         inc=(sV, 1))
    B.mark("bias_all", sV)
    B.bank_release(5, "bias_all")

    if upto == "norms":
        _finish_debug()
        return _playback_and_close()
    B.wait("scalar", "bias_all")

    def pv_emit(hp, qg, kc):
        for hh in range(2):
            B.wait("tensor", f"exp_{hp}_{qg}_{kc}_{hh}")
            if hp == 0 and qg == 0 and kc == 0 and hh == 0:
                B.wait("tensor", "vones")
            qs = max(qg * 512, kc * 128)
            for qt in range(max(kc, 4 * qg), 4 * qg + 4):
                j = qt - 4 * qg
                co = (j % 2) * 130 + hh * 65
                B.op("tensor", "matmul",
                     bank[4 + j // 2][:, co:co + 65],
                     piece[0][hh][kc % 2][:,
                         qt * 128 - qs:qt * 128 - qs + 128],
                     Vtm[kc][:, (2 * hp + hh) * 65:(2 * hp + hh) * 65 + 65],
                     start=False, stop=True, skip_group_check=True,
                     inc=(sPE, 1) if qt == 4 * qg + 3 else None)
            B.mark(f"pv_{hp}_{qg}_{kc}_{hh}", sPE)

    for hp in range(NCK):
        for qg in range(4):
            q0 = qg * 512
            pvb0 = 4
            for half in range(2):
                i = pvb0 + half
                B.bank_acquire("tensor", i)
                B.op("tensor", "matmul", bank[i][:, :], zerosb[:, :],
                     Qfm[0][:, 0:512], start=True, stop=True,
                     skip_group_check=True, inc=(sPE, 1))
                B.mark(f"pz_{hp}_{qg}_{half}", sPE)
            for kc in range(4 * qg + 4):
                qs = max(q0, kc * 128)
                w = q0 + 512 - qs
                diag = kc >= 4 * qg
                par = kc % 2
                for hh in range(2):
                    i = hh * 2 + par
                    B.bank_acquire("tensor", i)
                    B.op("tensor", "matmul", bank[i][:, 0:w],
                         Kfm[hp][hh * 64:(hh + 1) * 64,
                                 kc * 128:(kc + 1) * 128],
                         Qfm[hp][hh * 64:(hh + 1) * 64, qs:qs + w],
                         start=True, stop=not diag,
                         tile_position=(hh * 64, 0),
                         inc=None if diag else (sPE, 1))
                    if diag:
                        B.op("tensor", "matmul", bank[i][:, 0:128],
                             identb[:, :], dmaskb[:, :],
                             start=False, stop=True, inc=(sPE, 1))
                    B.mark(f"sc_{hp}_{qg}_{kc}_{hh}", sPE)
                    B.wait("scalar", f"sc_{hp}_{qg}_{kc}_{hh}")
                    if kc >= 2:
                        B.wait("scalar", f"pv_{hp}_{qg}_{kc - 2}_{hh}")
                    elif qg > 0:
                        B.wait("scalar",
                               f"pv_{hp}_{qg - 1}_{4 * qg - 2 + kc}_{hh}")
                    elif hp > 0:
                        B.wait("scalar", f"pv_{hp - 1}_3_{14 + kc}_{hh}")
                    B.op("scalar", "activation",
                         piece[0][hh][par][:, 0:w], bank[i][:, 0:w], AF.Exp,
                         bias=bias_hd[:, 2 * hp + hh:2 * hp + hh + 1],
                         scale=0.125, inc=(sA, 1))
                    B.mark(f"exp_{hp}_{qg}_{kc}_{hh}", sA)
                    B.bank_release(i, f"exp_{hp}_{qg}_{kc}_{hh}")
                if kc >= 1:
                    pv_emit(hp, qg, kc - 1)
            pv_emit(hp, qg, 4 * qg + 3)
            B.wait("vector", f"pv_{hp}_{qg}_{4 * qg + 3}_1")
            if hp >= 1:
                B.wait("vector", f"atTm_{hp - 1}_3")
            for j in range(4):
                i2 = pvb0 + j // 2
                co = (j % 2) * 130
                for hh in range(2):
                    B.op("vector", "reciprocal",
                         rcsc[:, 2 * j + hh:2 * j + hh + 1],
                         bank[i2][:, co + hh * 65 + 64:co + hh * 65 + 65])
            for j in range(4):
                qt = 4 * qg + j
                i2 = pvb0 + j // 2
                co = (j % 2) * 130
                for hh in range(2):
                    B.op("vector", "tensor_scalar",
                         attn_tm[0][:, qt * 128 + hh * 64:
                                     qt * 128 + hh * 64 + 64],
                         bank[i2][:, co + hh * 65:co + hh * 65 + 64],
                         rcsc[:, 2 * j + hh:2 * j + hh + 1], None,
                         op0=OP.mult,
                         inc=(sV, 1) if (j % 2 == 1 and hh == 1) else None)
                if j % 2 == 1:
                    B.mark(f"pvev_{hp}_{qg}_{j // 2}", sV)
                    B.bank_release(i2, f"pvev_{hp}_{qg}_{j // 2}")
        B.wait("tensor", f"pvev_{hp}_3_1")
        for qt in range(NTT):
            i = (qt // 4) % 4
            if qt % 4 == 0:
                B.bank_acquire("tensor", i)
            B.op("tensor", "transpose",
                 bb(i)[:, (qt % 4) * 128:(qt % 4) * 128 + 128],
                 attn_tm[0][:, qt * 128:(qt + 1) * 128], identb[:, :],
                 inc=(sPE, 1) if qt % 4 == 3 else None)
            if qt % 4 == 3:
                B.mark(f"atTm_{hp}_{qt // 4}", sPE)
                B.wait("vector", f"atTm_{hp}_{qt // 4}")
                B.op("vector", "tensor_copy",
                     afm[hp][:, (qt - 3) * 128:(qt + 1) * 128],
                     bb(i)[:, 0:512], inc=(sV, 1))
                B.mark(f"afmev_{hp}_{qt // 4}", sV)
                B.bank_release(i, f"afmev_{hp}_{qt // 4}")
    B.mark("attn_done", sV)

    if upto == "attn":
        _finish_debug()
        return _playback_and_close()
    B.wait("tensor", "attn_done")
    B.wait("tensor", "wo")
    B.wait("sync", "kn")
    for tt in range(NTT):
        if tt >= 2:
            B.wait("sync", f"h1_{tt - 2}")
        s_dma(xstage[tt % 2][:, :], dx[tt * 128:(tt + 1) * 128, :],
              ev=f"xf32_{tt}")
    def emit_x1T(tt):
        B.wait("tensor", f"x1b_{tt}")
        for c in range(NCK):
            i = 4 + c // 4
            if c % 4 == 0:
                B.bank_acquire("tensor", i)
            B.op("tensor", "transpose",
                 bb(i)[:, (c % 4) * 128:(c % 4) * 128 + 128],
                 x1bfst[tt % 2][:, c * 128:(c + 1) * 128], identb[:, :],
                 inc=(sPE, 1) if c == NCK - 1 else None)
        B.mark(f"x1T_{tt}", sPE)
        B.wait("vector", f"x1T_{tt}")
        for half in range(2):
            i = 4 + half
            B.op("vector", "tensor_copy",
                 A_v[:, 0:16384].rearrange("p (c t) -> p c t", c=8)
                 [:, half * 4:half * 4 + 4, tt * 128:(tt + 1) * 128],
                 bb(i)[:, 0:512].rearrange("p (c t) -> p c t", c=4),
                 inc=(sV, 1))
            B.mark(f"x1fev_{tt}_{half}", sV)
            B.bank_release(i, f"x1fev_{tt}_{half}")

    bi = 0
    for tt in range(NTT):
        for dog in range(2):
            i = bi % 4
            bk = bank[i]
            B.bank_acquire("tensor", i)
            for dc in range(NCK):
                B.op("tensor", "matmul", bk[:, :],
                     afm[dc][:, tt * 128:(tt + 1) * 128],
                     wslot[1][:, dc, dog * 512:(dog + 1) * 512],
                     start=(dc == 0), stop=False)
            B.op("tensor", "matmul", bk[:, :], ones_b[:, :],
                 outb_row[0:1, dog * 512:(dog + 1) * 512],
                 start=False, stop=True, inc=(sPE, 1))
            B.mark(f"omm{bi}", sPE)
            B.wait("vector", f"omm{bi}")
            if dog == 0:
                B.wait("vector", f"xf32_{tt}")
            if dbg_nores:
                B.op("vector", "tensor_scalar",
                     x1_tm[tt][:, dog * 512:(dog + 1) * 512], bk[:, :],
                     1.0, 0.0, op0=OP.mult, op1=OP.add,
                     accum_out=lnst[:, dog:dog + 1], inc=(sV, 1))
            else:
                B.op("vector", "scalar_tensor_tensor",
                     x1_tm[tt][:, dog * 512:(dog + 1) * 512], bk[:, :], 1.0,
                     xstage[tt % 2][:, dog * 512:(dog + 1) * 512],
                     op0=OP.mult, op1=OP.add,
                     accum_out=lnst[:, dog:dog + 1],
                     inc=(sV, 1))
            B.mark(f"h1ev{bi}", sV)
            B.bank_release(i, f"h1ev{bi}")
            bi += 1
        B.mark(f"h1_{tt}", sV)
        if tt >= 1:
            emit_x1T(tt - 1)
        if dbg_h1 and tt < 2:
            dh = nc.dram_tensor(f"dbg_h1_{tt}", [128, D], F32,
                                kind="ExternalOutput").ap()
            dxs = nc.dram_tensor(f"dbg_xs_{tt}", [128, D], F32,
                                 kind="ExternalOutput").ap()
            B.wait("scalar", f"h1_{tt}")
            B.op("scalar", "dma_start", out=dh, in_=x1_tm[tt][:, :],
                 inc=(sDs, 16))
            B.op("scalar", "dma_start", out=dxs, in_=xstage[tt % 2][:, :],
                 inc=(sDs, 16))
            B.wait_val("vector", sDs, sDs.n)
        B.wait("scalar", f"h1_{tt}")
        B.op("scalar", "activation", x1bfst[tt % 2][:, :], x1_tm[tt][:, :],
             AF.Square, accum_out=lnst[:, 2:3], inc=(sA, 1))
        B.mark(f"sq1_{tt}", sA)
        B.wait("vector", f"sq1_{tt}")
        B.op("vector", "tensor_reduce", lnst[:, 3:4], lnst[:, 0:2], axis=AX.X,
             op=OP.add)
        B.op("vector", "tensor_scalar", lnst[:, 3:4], lnst[:, 3:4], 1.0 / D,
             None, op0=OP.mult)
        B.op("vector", "tensor_scalar", lnst[:, 4:5], lnst[:, 3:4],
             lnst[:, 3:4], None, op0=OP.mult)
        B.op("vector", "scalar_tensor_tensor", lnst[:, 5:6], lnst[:, 2:3],
             1.0 / D, lnst[:, 4:5], op0=OP.mult, op1=OP.subtract)
        B.op("vector", "tensor_scalar", lnst[:, 5:6], lnst[:, 5:6], LN_EPS,
             None, op0=OP.add)
        B.op("vector", "reciprocal", lnst[:, 6:7], lnst[:, 5:6], inc=(sV, 1))
        B.mark(f"st1a_{tt}", sV)
        B.wait("scalar", f"st1a_{tt}")
        B.op("scalar", "activation", lnst[:, 7:8], lnst[:, 6:7], AF.Sqrt,
             inc=(sA, 1))
        B.mark(f"st1b_{tt}", sA)
        B.wait("vector", f"st1b_{tt}")
        if not dbg_nores:
            B.op("vector", "tensor_scalar", x1_tm[tt][:, :], x1_tm[tt][:, :],
                 lnst[:, 3:4], lnst[:, 7:8], op0=OP.subtract, op1=OP.mult)
        B.op("vector", "tensor_copy", x1bfst[tt % 2][:, :], x1_tm[tt][:, :],
             inc=(sV, 1))
        B.mark(f"x1b_{tt}", sV)
    emit_x1T(NTT - 1)
    B.mark("omm_done", sPE)

    if upto == "ln1":
        _finish_debug()
        return _playback_and_close()
    B.wait("tensor", "smallw")
    for tt in range(NTT):
        B.wait("tensor", f"x1fev_{tt}_1")
        i = 6
        B.bank_acquire("tensor", i)
        for dc in range(NCK):
            B.op("tensor", "matmul", bank[i][:, 0:NE],
                 x1fm[dc][:, tt * 128:(tt + 1) * 128],
                 gw_sb[:, dc * NE:(dc + 1) * NE],
                 start=(dc == 0), stop=False)
        B.op("tensor", "matmul", bank[i][:, 0:NE], ones_b[:, :],
             gb_row[0:1, :], start=False, stop=True, inc=(sPE, 1))
        B.mark(f"gmm_{tt}", sPE)
        B.wait("vector", f"gmm_{tt}")
        sc = scores_g[:, tt * NE:(tt + 1) * NE]
        B.op("vector", "tensor_scalar", sc, bank[i][:, 0:NE], EPS_GATE, None,
             op0=OP.max, inc=(sV, 1))
        B.mark(f"gev_{tt}", sV)
        B.bank_release(i, f"gev_{tt}")
        B.op("vector", "reduce_max", gsc[:, 0:1], sc, axis=AX.X, negate=True,
             inc=(sV, 1))
        B.mark(f"gmx_{tt}", sV)
        B.wait("scalar", f"gmx_{tt}")
        B.op("scalar", "activation", p_g[:, :], sc, AF.Exp,
             bias=gsc[:, 0:1], scale=1.0, accum_out=gsc[:, 1:2], inc=(sA, 1))
        B.mark(f"gexp_{tt}", sA)
        B.wait("vector", f"gexp_{tt}")
        B.op("vector", "reciprocal", gsc[:, 2:3], gsc[:, 1:2])
        B.op("vector", "tensor_scalar", p_g[:, :], p_g[:, :], gsc[:, 2:3],
             None, op0=OP.mult)
        B.op("vector", "reduce_max", gsc[:, 3:4], p_g[:, :], axis=AX.X,
             negate=True)
        B.op("vector", "tensor_scalar", g8a[:, :], p_g[:, :], gsc[:, 3:4],
             None, op0=OP.add)
        B.op("vector", "tensor_single_scalar", g8a[:, :], g8a[:, :], 0.0,
             OP.is_ge)
        B.op("vector", "tensor_scalar", g8a[:, :], g8a[:, :], -BIG, BIG,
             op0=OP.mult, op1=OP.add)
        B.op("vector", "tensor_tensor", g8a[:, :], g8a[:, :], idx8[:, :],
             op=OP.add)
        B.op("vector", "tensor_reduce", gsc[:, 4:5], g8a[:, :], axis=AX.X,
             op=OP.min, negate=True)
        B.op("vector", "tensor_scalar", g8a[:, :], idx8[:, :], gsc[:, 4:5],
             None, op0=OP.add)
        B.op("vector", "tensor_single_scalar", g8a[:, :], g8a[:, :], 0.0,
             OP.is_equal)
        B.op("vector", "scalar_tensor_tensor", g8b[:, :], g8a[:, :], -BIG,
             p_g[:, :], op0=OP.mult, op1=OP.add)
        B.op("vector", "reduce_max", gsc[:, 5:6], g8b[:, :], axis=AX.X,
             negate=True)
        B.op("vector", "tensor_scalar", g8b[:, :], g8b[:, :], gsc[:, 5:6],
             None, op0=OP.add)
        B.op("vector", "tensor_single_scalar", g8b[:, :], g8b[:, :], 0.0,
             OP.is_ge)
        B.op("vector", "tensor_scalar", g8b[:, :], g8b[:, :], -BIG, BIG,
             op0=OP.mult, op1=OP.add)
        B.op("vector", "tensor_tensor", g8b[:, :], g8b[:, :], idx8[:, :],
             op=OP.add)
        B.op("vector", "tensor_reduce", gsc[:, 6:7], g8b[:, :], axis=AX.X,
             op=OP.min, negate=True)
        B.op("vector", "tensor_scalar", g8b[:, :], idx8[:, :], gsc[:, 6:7],
             None, op0=OP.add)
        B.op("vector", "tensor_single_scalar", g8b[:, :], g8b[:, :], 0.0,
             OP.is_equal)
        B.op("vector", "tensor_tensor", g8a[:, :], g8a[:, :], g8b[:, :],
             op=OP.add)
        cb = comb_g[:, tt * NE:(tt + 1) * NE]
        B.op("vector", "tensor_scalar", cb, p_g[:, :], W_THRES, None,
             op0=OP.max)
        B.op("vector", "tensor_tensor", cb, cb, g8a[:, :], op=OP.mult)
        B.op("vector", "tensor_reduce", gsc[:, 7:8], cb, axis=AX.X, op=OP.add)
        B.op("vector", "reciprocal", gsc[:, 8:9], gsc[:, 7:8])
        B.op("vector", "tensor_scalar", cb, cb, gsc[:, 8:9], 0.5,
             op0=OP.mult, op1=OP.mult)
        B.op("vector", "tensor_copy", combbf[:, tt * NE:(tt + 1) * NE], cb,
             inc=(sV, 1))
        B.mark(f"comb_{tt}", sV)

    if upto == "gate":
        _finish_debug()
        return _playback_and_close()
    def emit_combT():
        B.wait("tensor", "smallw")
        for tt in range(NTT):
            B.wait("tensor", f"comb_{tt}")
            i = 6 + tt % 2
            B.bank_acquire("tensor", i)
            B.op("tensor", "transpose", bb(i)[0:NE, 0:128],
                 combbf[:, tt * NE:(tt + 1) * NE], identb[:, :],
                 inc=(sPE, 1))
            B.mark(f"combT_{tt}", sPE)
            B.wait("vector", f"combT_{tt}")
            B.op("vector", "tensor_copy", combT[:, tt * 128:(tt + 1) * 128],
                 bb(i)[0:NE, 0:128], inc=(sV, 1))
            B.mark(f"combTev_{tt}", sV)
            B.bank_release(i, f"combTev_{tt}")

    bi = 0
    for e in range(NE):
        B.wait("tensor", f"we{e}")
        ws = wslot[e % 2]
        for tt in range(NTT):
            for dog in range(2):
                i = bi % 6
                B.bank_acquire("tensor", i)
                for dc in range(NCK):
                    B.op("tensor", "matmul", bank[i][:, :],
                         x1fm[dc][:, tt * 128:(tt + 1) * 128],
                         ws[:, dc, dog * 512:(dog + 1) * 512],
                         start=(dc == 0), stop=(dc == NCK - 1),
                         inc=(sPE, 1) if dc == NCK - 1 else None)
                B.mark(f"emm{bi}", sPE)
                B.wait("vector", f"emm{bi}")
                if e == 0:
                    B.op("vector", "tensor_scalar",
                         acc[tt][:, dog * 512:(dog + 1) * 512], bank[i][:, :],
                         comb_g[:, tt * NE + e:tt * NE + e + 1], None,
                         op0=OP.mult, inc=(sV, 1))
                else:
                    B.op("vector", "scalar_tensor_tensor",
                         acc[tt][:, dog * 512:(dog + 1) * 512], bank[i][:, :],
                         comb_g[:, tt * NE + e:tt * NE + e + 1],
                         acc[tt][:, dog * 512:(dog + 1) * 512],
                         op0=OP.mult, op1=OP.add, inc=(sV, 1))
                B.mark(f"eev{bi}", sV)
                B.bank_release(i, f"eev{bi}")
                bi += 1
        B.mark(f"expmm{e}", sPE)
        if e == 0:
            emit_combT()
    for tt in range(NTT):
        for dog in range(2):
            i = bi % 6
            B.bank_acquire("tensor", i)
            B.op("tensor", "matmul", bank[i][:, :],
                 combT[:, tt * 128:(tt + 1) * 128],
                 eb_sb[:, dog * 512:(dog + 1) * 512],
                 start=True, stop=True, inc=(sPE, 1))
            B.mark(f"emm{bi}", sPE)
            B.wait("vector", f"emm{bi}")
            B.op("vector", "scalar_tensor_tensor",
                 acc[tt][:, dog * 512:(dog + 1) * 512], bank[i][:, :], 1.0,
                 acc[tt][:, dog * 512:(dog + 1) * 512],
                 op0=OP.mult, op1=OP.add, inc=(sV, 1))
            B.mark(f"eev{bi}", sV)
            B.bank_release(i, f"eev{bi}")
            bi += 1

    if upto == "ffn":
        _finish_debug()
        return _playback_and_close()
    for tt in range(NTT):
        B.op("vector", "scalar_tensor_tensor", acc[tt][:, 0:512],
             acc[tt][:, 0:512], 1.0, x1_tm[tt][:, 0:512],
             op0=OP.mult, op1=OP.add, accum_out=lnst[:, 0:1])
        B.op("vector", "scalar_tensor_tensor", acc[tt][:, 512:1024],
             acc[tt][:, 512:1024], 1.0, x1_tm[tt][:, 512:1024],
             op0=OP.mult, op1=OP.add, accum_out=lnst[:, 1:2], inc=(sV, 1))
        B.mark(f"h2_{tt}", sV)
        B.wait("scalar", f"h2_{tt}")
        if tt == 0:
            B.bank_acquire("scalar", 6)
            B.bank_acquire("scalar", 7)
        B.op("scalar", "activation", bank[6][:, :], acc[tt][:, 0:512],
             AF.Square, accum_out=lnst[:, 2:3])
        B.op("scalar", "activation", bank[7][:, :], acc[tt][:, 512:1024],
             AF.Square, accum_out=lnst[:, 4:5], inc=(sA, 1))
        B.mark(f"sq2_{tt}", sA)
        B.wait("vector", f"sq2_{tt}")
        B.op("vector", "tensor_reduce", lnst[:, 3:4], lnst[:, 0:2], axis=AX.X,
             op=OP.add)
        B.op("vector", "tensor_scalar", lnst[:, 3:4], lnst[:, 3:4], 1.0 / D,
             None, op0=OP.mult)
        B.op("vector", "tensor_tensor", lnst[:, 2:3], lnst[:, 2:3],
             lnst[:, 4:5], op=OP.add)
        B.op("vector", "tensor_scalar", lnst[:, 4:5], lnst[:, 3:4],
             lnst[:, 3:4], None, op0=OP.mult)
        B.op("vector", "scalar_tensor_tensor", lnst[:, 5:6], lnst[:, 2:3],
             1.0 / D, lnst[:, 4:5], op0=OP.mult, op1=OP.subtract)
        B.op("vector", "tensor_scalar", lnst[:, 5:6], lnst[:, 5:6], LN_EPS,
             None, op0=OP.add)
        B.op("vector", "reciprocal", lnst[:, 6:7], lnst[:, 5:6], inc=(sV, 1))
        B.mark(f"st2a_{tt}", sV)
        B.wait("scalar", f"st2a_{tt}")
        B.op("scalar", "activation", lnst[:, 7:8], lnst[:, 6:7], AF.Sqrt,
             inc=(sA, 1))
        B.mark(f"st2b_{tt}", sA)
        B.wait("vector", f"st2b_{tt}")
        B.op("vector", "tensor_scalar", acc[tt][:, :], acc[tt][:, :],
             lnst[:, 3:4], lnst[:, 7:8], op0=OP.subtract, op1=OP.mult,
             inc=(sV, 1))
        B.mark(f"out_{tt}", sV)
        B.wait("sync", f"out_{tt}")
        s_dma(dy[tt * 128:(tt + 1) * 128, :], acc[tt][:, :], ev=f"st_{tt}")
    B.wait_val("sync", sDs, sDs.n)

    return _playback_and_close()


_NC_CACHE = {}


def _get_nc():
    if "nc" not in _NC_CACHE:
        _NC_CACHE["nc"] = build_program()
    return _NC_CACHE["nc"]


def _consts():
    identf = np.eye(128, dtype=np.float32)
    identb = identf.astype(bf16)
    k = np.arange(128)[:, None]
    q = np.arange(128)[None, :]
    dmask = np.where(k > q, NEG, 0.0).astype(np.float32).astype(bf16)
    sel = np.zeros((128, 128), np.float32)
    for c in range(8):
        sel[0:64, c * 16 + 2 * c] = 1.0
        sel[64:128, c * 16 + 2 * c + 1] = 1.0
    selb = sel.astype(bf16)
    idx8 = np.broadcast_to(np.arange(8, dtype=np.float32), (128, 8)).copy()
    return identf, identb, dmask, selb, idx8


def make_in_maps(x, wq, wk, wv, wo, outb, gw, gb, ew, eb):
    identf, identb, dmask, selb, idx8 = _consts()
    in_maps = []
    for i in range(8):
        in_maps.append({
            "x": np.ascontiguousarray(x[i]),
            "wq": wq, "wk": wk, "wv": wv, "wo": wo, "outb": outb,
            "gw": gw, "gb": gb, "ew": ew, "eb": eb,
            "identf": identf, "identb": identb, "dmask": dmask,
            "sel": selb, "idx8": idx8,
        })
    return in_maps


def kernel(L0_attn_batch, L0_attn_w_query, L0_attn_w_key, L0_attn_w_value,
           L0_attn_out_w, L0_attn_out_b, L0_ffn_gate_W, L0_ffn_gate_b,
           L0_ffn_expert_W, L0_ffn_expert_b):
    x = np.ascontiguousarray(np.asarray(L0_attn_batch, dtype=np.float32))
    wq = np.ascontiguousarray(np.asarray(L0_attn_w_query, dtype=np.float32))
    wk = np.ascontiguousarray(np.asarray(L0_attn_w_key, dtype=np.float32))
    wv = np.ascontiguousarray(np.asarray(L0_attn_w_value, dtype=np.float32))
    wo = np.ascontiguousarray(np.asarray(L0_attn_out_w, dtype=np.float32))
    outb = np.ascontiguousarray(
        np.asarray(L0_attn_out_b, dtype=np.float32)).reshape(1, D)
    gw = np.ascontiguousarray(np.asarray(L0_ffn_gate_W, dtype=np.float32))
    gb = np.ascontiguousarray(
        np.asarray(L0_ffn_gate_b, dtype=np.float32)).reshape(1, NE)
    ew = np.ascontiguousarray(np.asarray(L0_ffn_expert_W, dtype=np.float32))
    eb = np.ascontiguousarray(np.asarray(L0_ffn_expert_b, dtype=np.float32))
    nc = _get_nc()
    in_maps = make_in_maps(x, wq, wk, wv, wo, outb, gw, gb, ew, eb)
    res = run_bass_kernel_spmd(nc, in_maps, core_ids=list(range(8)))
    out = np.stack([np.asarray(r["y"], dtype=np.float32)
                    for r in res.results], axis=0)
    return out
